# version 01
# baseline (speedup 1.0000x reference)
import contextlib
import numpy as np

import concourse.bass as bass
import concourse.mybir as mybir
from concourse.bass_utils import run_bass_kernel_spmd

F32 = mybir.dt.float32
BF16 = mybir.dt.bfloat16
AF = mybir.ActivationFunctionType
ALU = mybir.AluOpType
AX = mybir.AxisListType

NCORES = 8
D = 1024
KCH = 8
TOK = 2048
SUB = 1024
LN_EPS = 1e-5



class Pr:
    ENGINES = ("tensor", "vector", "scalar", "gpsimd", "sync")

    def __init__(self):
        self.nc = bass.Bass("TRN2", target_bir_lowering=False)
        self._stack = contextlib.ExitStack()
        self.sems = {}
        self.cnt = {}
        self.hw = {}
        self.block = None
        self.sb = {}
        self.ps = []

    def sbuf(self, name, shape, dtype):
        h = self._stack.enter_context(self.nc.sbuf_tensor(name, list(shape), dtype))
        self.sb[name] = h
        return h

    def const(self, data, name):
        return self.nc.inline_tensor(np.ascontiguousarray(data), name=name)

    def start(self, npsum=8):
        for i in range(npsum):
            self.ps.append(self._stack.enter_context(
                self.nc.psum_tensor(f"P{i}", [128, 512], F32)))
        for e in self.ENGINES:
            self.sems[e] = self._stack.enter_context(self.nc.semaphore("sp_" + e))
            self.cnt[e] = 0
        for dname in ("dS", "dG"):
            self.sems[dname] = self._stack.enter_context(self.nc.semaphore("sm_" + dname))
            self.cnt[dname] = 0
        self.block = self._stack.enter_context(self.nc.Block())

    def sec(self, ename, body):
        pr = self

        def f(eng):
            body(_Emit(pr, ename, eng))

        getattr(self.block, ename)(f)

    def now(self, ename):
        return (ename, self.cnt[ename])

    def fence(self):
        toks = [self.now(e) for e in self.ENGINES] + [self.now("dS"), self.now("dG")]

        for e in self.ENGINES:
            def f(E, toks=toks):
                E.w(*toks)
            self.sec(e, f)

    def finish(self):
        def f(E):
            E.w(self.now("dS"), self.now("dG"))
        self.sec("sync", f)
        return self.nc


class _Emit:
    def __init__(self, pr, ename, eng):
        self.pr = pr
        self.en = ename
        self.e = eng

    def w(self, *tokens):
        for t in tokens:
            if t is None:
                continue
            sname, val = t
            if val <= 0:
                continue
            if sname == self.en and val <= self.pr.cnt[sname]:
                pass
            key = (self.en, sname)
            if self.pr.hw.get(key, 0) < val:
                self.e.wait_ge(self.pr.sems[sname], val)
                self.pr.hw[key] = val

    def t(self, instr):
        instr.then_inc(self.pr.sems[self.en], 1)
        self.pr.cnt[self.en] += 1
        return (self.en, self.pr.cnt[self.en])

    def dma(self, out, in_, sem=None, **kw):
        sem = sem or ("dG" if self.en == "gpsimd" else "dS")
        i = self.e.dma_start(out=out, in_=in_, **kw)
        i.then_inc(self.pr.sems[sem], 16)
        self.pr.cnt[sem] += 16
        return (sem, self.pr.cnt[sem])


def rearr_kp(ap):
    return ap.rearrange("(k p) n -> p k n", p=128)


def sb3(tile_ap, k):
    return tile_ap.rearrange("p (k n) -> p k n", k=k)



def emit_consts(pr):
    eye_c = pr.const(np.eye(128, dtype=np.float32), "eye_c")
    pr.sbuf("ones128", [128, 1], F32)
    pr.sbuf("ones128b", [128, 1], BF16)
    pr.sbuf("ones1r", [1, 128], F32)
    pr.sbuf("eye", [128, 128], F32)
    pr.sbuf("eyeb", [128, 128], BF16)
    tok = [None]

    def f(E):
        t1 = E.dma(pr.sb["eye"][:, :], eye_c.ap(), sem="dG")
        t1b = E.dma(pr.sb["eyeb"][:, :], eye_c.ap(), sem="dG")
        t2 = E.t(E.e.memset(pr.sb["ones128"][:, :], 1.0))
        t2b = E.t(E.e.memset(pr.sb["ones128b"][:, :], 1.0))
        t3 = E.t(E.e.memset(pr.sb["ones1r"][:, :], 1.0))
        tok[0] = t3
    pr.sec("gpsimd", f)
    return [tok[0], ("dG", pr.cnt["dG"])]


def emit_ln(pr, xT, xln, n_tok, dep_toks, name, gamma_beta=None):
    nt = n_tok // 512
    ones = pr.sb["ones128"]
    onesr = pr.sb["ones1r"]
    sq = pr.sb[name + "_sq"]
    rows = pr.sb[name + "_rows"]
    tmp = pr.sb[name + "_tmp"]
    P = pr.ps
    out_toks = []

    for t in range(nt):
        col = lambda k, t=t: slice(k * n_tok + t * 512, k * n_tok + (t + 1) * 512)
        sq_tok = [None] * KCH
        qmm = [None] * KCH
        smm = [None]
        rd = [None]
        bc = [None]

        def f_smm(E, t=t, smm=smm):
            E.w(*dep_toks)
            for k in range(KCH):
                i = E.e.matmul(P[0][0:1, :], ones[:, :], xT[:, col(k)],
                               start=(k == 0), stop=(k == KCH - 1))
            smm[0] = E.t(i)
        pr.sec("tensor", f_smm)

        for k in range(KCH):
            def f_sqk(E, t=t, k=k, sq_tok=sq_tok, qmm=qmm):
                E.w(*dep_toks)
                if k >= 2:
                    E.w(qmm[k - 2])
                half = sq[:, (k % 2) * 512:(k % 2) * 512 + 512]
                sq_tok[k] = E.t(E.e.activation(half, xT[:, col(k)], AF.Square))
            pr.sec("scalar", f_sqk)

            def f_qmmk(E, t=t, k=k, sq_tok=sq_tok, qmm=qmm):
                E.w(sq_tok[k])
                half = sq[:, (k % 2) * 512:(k % 2) * 512 + 512]
                i = E.e.matmul(P[1][0:1, :], pr.sb["ones128b"][:, :], half,
                               start=(k == 0), stop=(k == KCH - 1))
                qmm[k] = E.t(i)
            pr.sec("tensor", f_qmmk)

        def f_r1(E, t=t, qmm=qmm, smm=smm, rd=rd):
            m = rows[:, 0:512]; v = rows[:, 512:1024]; b = rows[:, 1024:1536]
            E.w(smm[0], qmm[KCH - 1])
            t1 = E.t(E.e.tensor_scalar_mul(m, P[0][0:1, :], 1.0 / D))
            t2 = E.t(E.e.tensor_scalar_mul(v, P[1][0:1, :], 1.0 / D))
            E.w(t1, t2)
            t3 = E.t(E.e.tensor_mul(b, m, m))
            E.w(t3)
            t4 = E.t(E.e.tensor_tensor(v, v, b, ALU.subtract))
            E.w(t4)
            t5 = E.t(E.e.tensor_scalar(v, v, LN_EPS, None, ALU.add))
            rd[0] = t5
        pr.sec("vector", f_r1)

        def f_r2(E, t=t, rd=rd):
            v = rows[:, 512:1024]
            E.w(rd[0])
            rd[0] = E.t(E.e.activation(v, v, AF.Sqrt))
        pr.sec("scalar", f_r2)

        def f_r3(E, t=t, rd=rd):
            m = rows[:, 0:512]; v = rows[:, 512:1024]; b = rows[:, 1024:1536]
            E.w(rd[0])
            t1 = E.t(E.e.reciprocal(v, v))
            E.w(t1)
            t2 = E.t(E.e.tensor_mul(b, m, v))
            E.w(t2)
            t3 = E.t(E.e.tensor_scalar_mul(b, b, -1.0))
            rd[0] = t3
        pr.sec("vector", f_r3)

        def f_bc(E, t=t, rd=rd, bc=bc):
            a = rows[:, 512:1024]; b = rows[:, 1024:1536]
            E.w(rd[0])
            if out_toks:
                E.w(out_toks[-1])
            E.e.matmul(P[2][:, :], onesr[:, :], a, start=True, stop=True)
            i = E.e.matmul(P[3][:, :], onesr[:, :], b, start=True, stop=True)
            bc[0] = E.t(i)
        pr.sec("tensor", f_bc)

        def f_ap(E, t=t, bc=bc):
            E.w(bc[0])
            last = None
            for k in range(KCH):
                t1 = E.t(E.e.tensor_tensor(tmp[:, :], xT[:, col(k)], P[2][:, :], ALU.mult))
                E.w(t1)
                if gamma_beta is None:
                    last = E.t(E.e.tensor_tensor(xln[:, col(k)], tmp[:, :], P[3][:, :], ALU.add))
                else:
                    last = E.t(E.e.tensor_tensor(tmp[:, :], tmp[:, :], P[3][:, :], ALU.add))
                    E.w(last)
                    g_sb, b_sb = gamma_beta
                    last = E.t(E.e.tensor_scalar(xln[:, col(k)], tmp[:, :],
                                                 g_sb[:, k:k + 1], b_sb[:, k:k + 1],
                                                 ALU.mult, ALU.add))
                E.w(last)
            out_toks.append(last)
        pr.sec("vector", f_ap)

    return out_toks


def alloc_ln(pr, name):
    pr.sbuf(name + "_sq", [128, 1024], BF16)
    pr.sbuf(name + "_rows", [1, 1536], F32)
    pr.sbuf(name + "_tmp", [128, 512], F32)



def build_ffn0():
    pr = Pr()
    nc = pr.nc
    xT_d = nc.dram_tensor("xT", [D, TOK], F32, kind="ExternalInput").ap()
    gwT_d = nc.dram_tensor("gwT", [D, 8], F32, kind="ExternalInput").ap()
    w1T_d = nc.dram_tensor("w1T", [8, D, D], F32, kind="ExternalInput").ap()
    w2T_d = nc.dram_tensor("w2T", [8, D, D], F32, kind="ExternalInput").ap()
    w3T_d = nc.dram_tensor("w3T", [8, D, D], F32, kind="ExternalInput").ap()
    b1r_d = nc.dram_tensor("b1r", [128, 64], F32, kind="ExternalInput").ap()
    b2r_d = nc.dram_tensor("b2r", [128, 64], F32, kind="ExternalInput").ap()
    b3r_d = nc.dram_tensor("b3r", [128, 64], F32, kind="ExternalInput").ap()
    sw1T_d = nc.dram_tensor("sw1T", [D, 2048], F32, kind="ExternalInput").ap()
    sw2T_d = nc.dram_tensor("sw2T", [D, 2048], F32, kind="ExternalInput").ap()
    sw3T_d = nc.dram_tensor("sw3T", [2048, D], F32, kind="ExternalInput").ap()
    yT_d = nc.dram_tensor("yT", [D, TOK], F32, kind="ExternalOutput").ap()

    sel = np.zeros((8, 8 * 128), dtype=np.float32)
    for e in range(8):
        sel[e, e * 128:(e + 1) * 128] = 1.0
    sel_c = pr.const(sel, "sel8_c")

    xT = pr.sbuf("sxT", [128, KCH * SUB], F32)
    xln = pr.sbuf("xln", [128, KCH * SUB], BF16)
    Y = pr.sbuf("Y", [128, KCH * SUB], F32)
    wpool = pr.sbuf("wpool", [128, 16 * D], BF16)
    xs = pr.sbuf("xs", [128, KCH * SUB], BF16)
    hb = pr.sbuf("hb", [128, KCH * SUB], BF16)
    x3sb = pr.sbuf("x3sb", [128, KCH * SUB], BF16)
    pr.sbuf("sel8", [8, 1024], BF16)
    pr.sbuf("gw", [128, KCH * 8], BF16)
    pr.sbuf("b1", [128, 64], F32)
    pr.sbuf("b2", [128, 64], F32)
    pr.sbuf("b3", [128, 64], F32)
    pr.sbuf("WEr", [8, SUB], BF16)
    pr.sbuf("rt", [128, 48], F32)
    pr.sbuf("rt2", [128, 8], F32)
    pr.sbuf("webc", [128, 2 * 512], BF16)
    pr.sbuf("mebc", [128, 2 * 512], BF16)
    pr.sbuf("x2", [128, 512], BF16)
    pr.sbuf("prod", [128, 512], BF16)
    alloc_ln(pr, "ffn")
    pr.start()

    ctoks = emit_consts(pr)

    def f_init(E):
        t1 = E.dma(pr.sb["sel8"][:, :], sel_c.ap(), sem="dG")
        t2 = E.dma(sb3(pr.sb["gw"][:, :], KCH), rearr_kp(gwT_d), sem="dG")
    pr.sec("gpsimd", f_init)

    def f_init2(E):
        E.dma(pr.sb["b1"][:, :], b1r_d)
        E.dma(pr.sb["b2"][:, :], b2r_d)
        E.dma(pr.sb["b3"][:, :], b3r_d)
    pr.sec("sync", f_init2)
    ctoks += [("dG", pr.cnt["dG"]), ("dS", pr.cnt["dS"])]

    for sub in range(TOK // SUB):
        _ffn0_sub(pr, sub, xT_d, yT_d, w1T_d, w2T_d, w3T_d,
                  sw1T_d, sw2T_d, sw3T_d, ctoks)

    return pr.finish()


def _ffn0_sub(pr, sub, xT_d, yT_d, w1T_d, w2T_d, w3T_d, sw1T_d, sw2T_d, sw3T_d, ctoks):
    P = pr.ps
    xT = pr.sb["sxT"]; xln = pr.sb["xln"]; Y = pr.sb["Y"]
    wpool = pr.sb["wpool"]; xs = pr.sb["xs"]; hb = pr.sb["hb"]; x3sb = pr.sb["x3sb"]
    WEr = pr.sb["WEr"]; rt = pr.sb["rt"]; rt2 = pr.sb["rt2"]
    webc = pr.sb["webc"]; mebc = pr.sb["mebc"]; x2 = pr.sb["x2"]; prod = pr.sb["prod"]
    b1 = pr.sb["b1"]; b2 = pr.sb["b2"]; b3 = pr.sb["b3"]
    tok0 = sub * SUB
    nt = SUB // 512

    pr.fence()

    ld = [None]

    def f_ldx(E):
        src = rearr_kp(xT_d)
        ld[0] = E.dma(sb3(xT[:, :], KCH)[:, :, 0:SUB] if tok0 == 0
                      else sb3(xT[:, :], KCH)[:, :, 0:SUB],
                      src[:, :, tok0:tok0 + SUB])
    pr.sec("sync", f_ldx)

    def f_yinit(E):
        E.w(ld[0])
        for k in range(KCH):
            E.t(E.e.tensor_copy(Y[:, k * SUB:(k + 1) * SUB], xT[:, k * SUB:(k + 1) * SUB]))
    pr.sec("gpsimd", f_yinit)

    ln_toks = emit_ln(pr, xT, xln, SUB, [ld[0]] + ctoks, "ffn")

    tr_last = [None]
    for tt in range(SUB // 128):
        tile_ln = ln_toks[(tt * 128) // 512]
        st = [None]

        def f_rmm(E, tt=tt, tile_ln=tile_ln, st=st):
            E.w(tile_ln, *ctoks)
            E.w(self_tok(pr, "vector"))
            for k in range(KCH):
                i = E.e.matmul(P[0][:, 0:8],
                               xln[:, k * SUB + tt * 128: k * SUB + (tt + 1) * 128],
                               pr.sb["gw"][:, k * 8:(k + 1) * 8],
                               start=(k == 0), stop=(k == KCH - 1))
            st[0] = E.t(i)
        pr.sec("tensor", f_rmm)

        def f_rv(E, tt=tt, st=st):
            m = rt2[:, 0:1]; nm = rt2[:, 1:2]
            E.w(st[0])
            t1 = E.t(E.e.reduce_max(m, P[0][:, 0:8], axis=AX.X))
            E.w(t1)
            t2 = E.t(E.e.tensor_scalar_mul(nm, m, -1.0))
            E.w(t2)
            st[0] = t2
        pr.sec("vector", f_rv)

        def f_rs(E, tt=tt, st=st):
            ex = rt[:, 8:16]; nm = rt2[:, 1:2]
            E.w(st[0])
            st[0] = E.t(E.e.activation(ex, P[0][:, 0:8], AF.Exp, bias=nm))
        pr.sec("scalar", f_rs)

        def f_rv2(E, tt=tt, st=st):
            s = rt[:, 0:8]; ex = rt[:, 8:16]; ge = rt[:, 16:24]
            we = rt[:, 24:32]; msk = rt[:, 32:40]
            ssum = rt2[:, 2:3]; rsum = rt2[:, 3:4]; m1 = rt2[:, 4:5]; m2 = rt2[:, 5:6]
            E.w(st[0])
            t1 = E.t(E.e.reduce_sum(ssum, ex, axis=AX.X))
            E.w(t1)
            t2 = E.t(E.e.reciprocal(rsum, ssum))
            E.w(t2)
            t3 = E.t(E.e.tensor_scalar_mul(s, ex, rsum))
            E.w(t3)
            t4 = E.t(E.e.reduce_max(m1, s, axis=AX.X))
            E.w(t4)
            t5 = E.t(E.e.tensor_single_scalar(ge, s, m1, ALU.is_ge))
            E.w(t5)
            t6 = E.t(E.e.scalar_tensor_tensor(msk, ge, -1e9, s, ALU.mult, ALU.add))
            E.w(t6)
            t7 = E.t(E.e.reduce_max(m2, msk, axis=AX.X))
            E.w(t7)
            t8 = E.t(E.e.tensor_single_scalar(ge, s, m2, ALU.is_ge))
            E.w(t8)
            t9 = E.t(E.e.tensor_mul(we, s, ge))
            st[0] = t9
        pr.sec("vector", f_rv2)

        def f_rtr(E, tt=tt, st=st):
            E.w(st[0])
            if tr_last[0] is not None:
                E.w(tr_last[0])
            i = E.e.transpose(P[1][0:8, 0:128], rt[:, 24:32], pr.sb["eye"][:, :])
            st[0] = E.t(i)
        pr.sec("tensor", f_rtr)

        def f_rcp(E, tt=tt, st=st):
            E.w(st[0])
            tr_last[0] = E.t(E.e.tensor_copy(WEr[:, tt * 128:(tt + 1) * 128], P[1][0:8, 0:128]))
        pr.sec("vector", f_rcp)

    pr.fence()

    for e in range(8):
        _ffn0_expert(pr, e, w1T_d, w2T_d, w3T_d)

    pr.fence()

    _ffn0_swiglu(pr, sw1T_d, sw2T_d, sw3T_d)

    pr.fence()

    def f_st(E):
        dst = rearr_kp(yT_d)
        E.dma(dst[:, :, tok0:tok0 + SUB], sb3(Y[:, :], KCH))
    pr.sec("sync", f_st)


def self_tok(pr, ename):
    return (ename, pr.cnt[ename])


def _ffn0_expert(pr, e, w1T_d, w2T_d, w3T_d):
    P = pr.ps
    nt = SUB // 512
    xln = pr.sb["xln"]; Y = pr.sb["Y"]; wpool = pr.sb["wpool"]
    xs = pr.sb["xs"]; hb = pr.sb["hb"]; x3sb = pr.sb["x3sb"]
    WEr = pr.sb["WEr"]; webc = pr.sb["webc"]; mebc = pr.sb["mebc"]
    x2 = pr.sb["x2"]; prod = pr.sb["prod"]
    b1 = pr.sb["b1"]; b2 = pr.sb["b2"]; b3 = pr.sb["b3"]

    wtok = [None]

    def f_ldw1(E, e=e):
        E.w(self_tok(pr, "tensor"))
        wtok[0] = E.dma(sb3(wpool[:, 0:KCH * D], KCH),
                        w1T_d[e, :, :].rearrange("(k p) h -> p k h", p=128), sem="dG")
    pr.sec("gpsimd", f_ldw1)

    for t in range(nt):
        bt = [None]

        def f_bc(E, e=e, t=t, bt=bt):
            E.w(self_tok(pr, "vector"))
            i = E.e.matmul(P[2][:, :], pr.sb["sel8"][:, e * 128:(e + 1) * 128],
                           WEr[:, t * 512:(t + 1) * 512], start=True, stop=True)
            bt[0] = E.t(i)
        pr.sec("tensor", f_bc)

        def f_bcc(E, t=t, bt=bt):
            wc = webc[:, t * 512:(t + 1) * 512]
            mc = mebc[:, t * 512:(t + 1) * 512]
            E.w(bt[0])
            t1 = E.t(E.e.tensor_copy(wc, P[2][:, :]))
            E.w(t1)
            t2 = E.t(E.e.tensor_single_scalar(mc, wc, 0.0, ALU.is_gt))
            E.w(t2)
            for k in range(KCH):
                E.t(E.e.tensor_mul(xs[:, k * SUB + t * 512: k * SUB + (t + 1) * 512],
                                   xln[:, k * SUB + t * 512: k * SUB + (t + 1) * 512], wc))
        pr.sec("vector", f_bcc)

    xs_done = self_tok(pr, "vector")

    silu_toks = {}
    for t in range(nt):
        for o in range(KCH):
            h1 = [None]

            def f_h1(E, t=t, o=o, h1=h1):
                E.w(wtok[0], xs_done)
                if o >= 2:
                    E.w(silu_toks[(t, o - 2)])
                elif t >= 1:
                    E.w(silu_toks[(t - 1, KCH - 2 + o)])
                for k in range(KCH):
                    i = E.e.matmul(P[4 + o % 2][:, :],
                                   wpool[:, k * D + o * 128: k * D + o * 128 + 128],
                                   xs[:, k * SUB + t * 512: k * SUB + (t + 1) * 512],
                                   start=(k == 0), stop=(k == KCH - 1))
                h1[0] = E.t(i)
            pr.sec("tensor", f_h1)

            def f_silu(E, e=e, t=t, o=o, h1=h1):
                E.w(h1[0])
                silu_toks[(t, o)] = E.t(E.e.activation(
                    hb[:, o * SUB + t * 512: o * SUB + (t + 1) * 512],
                    P[4 + o % 2][:, :], AF.Silu, bias=b1[:, e * 8 + o: e * 8 + o + 1]))
            pr.sec("scalar", f_silu)

    def f_ldw3(E, e=e):
        E.w(self_tok(pr, "tensor"))
        wtok[0] = E.dma(sb3(wpool[:, 0:KCH * D], KCH),
                        w3T_d[e, :, :].rearrange("(k p) h -> p k h", p=128), sem="dG")
    pr.sec("gpsimd", f_ldw3)

    x3_toks = {}
    for t in range(nt):
        for o in range(KCH):
            h3 = [None]

            def f_h3(E, t=t, o=o, h3=h3):
                E.w(wtok[0])
                if o >= 2:
                    E.w(x3_toks[(t, o - 2)])
                elif t >= 1:
                    E.w(x3_toks[(t - 1, KCH - 2 + o)])
                for k in range(KCH):
                    i = E.e.matmul(P[4 + o % 2][:, :],
                                   wpool[:, k * D + o * 128: k * D + o * 128 + 128],
                                   xs[:, k * SUB + t * 512: k * SUB + (t + 1) * 512],
                                   start=(k == 0), stop=(k == KCH - 1))
                h3[0] = E.t(i)
            pr.sec("tensor", f_h3)

            def f_x3(E, e=e, t=t, o=o, h3=h3):
                mc = mebc[:, t * 512:(t + 1) * 512]
                E.w(h3[0])
                x3_toks[(t, o)] = E.t(E.e.scalar_tensor_tensor(
                    x3sb[:, o * SUB + t * 512: o * SUB + (t + 1) * 512],
                    mc, b3[:, e * 8 + o: e * 8 + o + 1], P[4 + o % 2][:, :],
                    ALU.mult, ALU.add))
            pr.sec("vector", f_x3)

    def f_ldw2(E, e=e):
        E.w(self_tok(pr, "tensor"))
        wtok[0] = E.dma(sb3(wpool[:, 0:KCH * D], KCH),
                        w2T_d[e, :, :].rearrange("(k p) h -> p k h", p=128), sem="dG")
    pr.sec("gpsimd", f_ldw2)

    for t in range(nt):
        for o in range(KCH):
            h2 = [None]
            x2t = [None]
            pt = [None]

            def f_h2(E, t=t, o=o, h2=h2):
                E.w(wtok[0])
                E.w(self_tok(pr, "scalar"))
                E.w(self_tok(pr, "gpsimd"))
                for k in range(KCH):
                    i = E.e.matmul(P[4 + o % 2][:, :],
                                   wpool[:, k * D + o * 128: k * D + o * 128 + 128],
                                   hb[:, k * SUB + t * 512: k * SUB + (t + 1) * 512],
                                   start=(k == 0), stop=(k == KCH - 1))
                h2[0] = E.t(i)
            pr.sec("tensor", f_h2)

            def f_x2(E, e=e, t=t, o=o, h2=h2, x2t=x2t):
                E.w(h2[0])
                x2t[0] = E.t(E.e.activation(x2[:, :], P[4 + o % 2][:, :], AF.Identity,
                                            bias=b2[:, e * 8 + o: e * 8 + o + 1]))
            pr.sec("scalar", f_x2)

            def f_pr(E, t=t, o=o, x2t=x2t, pt=pt):
                E.w(x2t[0], x3_toks[(t, o)])
                t1 = E.t(E.e.tensor_mul(prod[:, :], x2[:, :],
                                        x3sb[:, o * SUB + t * 512: o * SUB + (t + 1) * 512]))
                E.w(t1)
                pt[0] = E.t(E.e.tensor_tensor(
                    Y[:, o * SUB + t * 512: o * SUB + (t + 1) * 512],
                    Y[:, o * SUB + t * 512: o * SUB + (t + 1) * 512],
                    prod[:, :], ALU.add))
            pr.sec("gpsimd", f_pr)


def _ffn0_swiglu(pr, sw1T_d, sw2T_d, sw3T_d):
    P = pr.ps
    nt = SUB // 512
    xln = pr.sb["xln"]; Y = pr.sb["Y"]; wpool = pr.sb["wpool"]
    G = pr.sb["sxT"].ap().bitcast(BF16)
    xs = pr.sb["xs"]; hb = pr.sb["hb"]

    wtok = [None]

    def f_ld1(E):
        E.w(self_tok(pr, "tensor"))
        wtok[0] = E.dma(sb3(wpool[:, :], KCH), rearr_kp(sw1T_d), sem="dG")
    pr.sec("gpsimd", f_ld1)

    g_toks = {}
    for t in range(nt):
        for o in range(16):
            gm = [None]

            def f_gmm(E, t=t, o=o, gm=gm):
                E.w(wtok[0])
                if o >= 2:
                    E.w(g_toks[(t, o - 2)])
                elif t >= 1:
                    E.w(g_toks[(t - 1, 14 + o)])
                for k in range(KCH):
                    i = E.e.matmul(P[4 + o % 2][:, :],
                                   wpool[:, k * 2048 + o * 128: k * 2048 + o * 128 + 128],
                                   xln[:, k * SUB + t * 512: k * SUB + (t + 1) * 512],
                                   start=(k == 0), stop=(k == KCH - 1))
                gm[0] = E.t(i)
            pr.sec("tensor", f_gmm)

            def f_gsilu(E, t=t, o=o, gm=gm):
                E.w(gm[0])
                g_toks[(t, o)] = E.t(E.e.activation(
                    G[:, o * SUB + t * 512: o * SUB + (t + 1) * 512],
                    P[4 + o % 2][:, :], AF.Silu))
            pr.sec("scalar", f_gsilu)

    def f_ld2(E):
        E.w(self_tok(pr, "tensor"))
        wtok[0] = E.dma(sb3(wpool[:, :], KCH), rearr_kp(sw2T_d), sem="dG")
    pr.sec("gpsimd", f_ld2)

    p_toks = {}
    for t in range(nt):
        for o in range(16):
            um = [None]

            def f_umm(E, t=t, o=o, um=um):
                E.w(wtok[0], self_tok(pr, "scalar"))
                if o >= 2:
                    E.w(p_toks[(t, o - 2)])
                elif t >= 1:
                    E.w(p_toks[(t - 1, 14 + o)])
                for k in range(KCH):
                    i = E.e.matmul(P[4 + o % 2][:, :],
                                   wpool[:, k * 2048 + o * 128: k * 2048 + o * 128 + 128],
                                   xln[:, k * SUB + t * 512: k * SUB + (t + 1) * 512],
                                   start=(k == 0), stop=(k == KCH - 1))
                um[0] = E.t(i)
            pr.sec("tensor", f_umm)

            def f_up(E, t=t, o=o, um=um):
                E.w(um[0])
                dst = xs if o < 8 else hb
                oo = o % 8
                p_toks[(t, o)] = E.t(E.e.tensor_tensor(
                    dst[:, oo * SUB + t * 512: oo * SUB + (t + 1) * 512],
                    G[:, o * SUB + t * 512: o * SUB + (t + 1) * 512],
                    P[4 + o % 2][:, :], ALU.mult))
            pr.sec("vector", f_up)

    def f_ld3(E):
        E.w(self_tok(pr, "tensor"))
        wtok[0] = E.dma(sb3(wpool[:, :], 16), rearr_kp(sw3T_d), sem="dG")
    pr.sec("gpsimd", f_ld3)

    zdone = {}
    for t in range(nt):
        for o in range(KCH):
            zm = [None]

            def f_zmm(E, t=t, o=o, zm=zm):
                E.w(wtok[0], self_tok(pr, "vector"))
                if o >= 2:
                    E.w(zdone[(t, o - 2)])
                elif t >= 1:
                    E.w(zdone[(t - 1, KCH - 2 + o)])
                for k in range(16):
                    srcb = xs if k < 8 else hb
                    kk = k % 8
                    i = E.e.matmul(P[4 + o % 2][:, :],
                                   wpool[:, k * D + o * 128: k * D + o * 128 + 128],
                                   srcb[:, kk * SUB + t * 512: kk * SUB + (t + 1) * 512],
                                   start=(k == 0), stop=(k == 15))
                zm[0] = E.t(i)
            pr.sec("tensor", f_zmm)

            def f_zacc(E, t=t, o=o, zm=zm):
                E.w(zm[0])
                zdone[(t, o)] = E.t(E.e.tensor_tensor(
                    Y[:, o * SUB + t * 512: o * SUB + (t + 1) * 512],
                    Y[:, o * SUB + t * 512: o * SUB + (t + 1) * 512],
                    P[4 + o % 2][:, :], ALU.add))
            pr.sec("vector", f_zacc)



_BUILD_CACHE = {}


def _get(name, builder):
    if name not in _BUILD_CACHE:
        _BUILD_CACHE[name] = builder()
    return _BUILD_CACHE[name]


def _shard_T(x_flat):
    out = []
    for c in range(NCORES):
        sh = x_flat[c * TOK:(c + 1) * TOK]
        out.append(np.ascontiguousarray(sh.T))
    return out


def _unshard_T(shards):
    return np.concatenate([s.T for s in shards], axis=0)


def _col_bias(b):
    return np.ascontiguousarray(b.reshape(8, 8, 128).transpose(2, 0, 1).reshape(128, 64))


def run_ffn0(x_flat, ins):
    nc = _get("ffn0", build_ffn0)
    shards = _shard_T(x_flat)
    base = {
        "gwT": np.ascontiguousarray(ins["L0_ffn_gate_w"].T),
        "w1T": np.ascontiguousarray(ins["L0_ffn_w1"].transpose(0, 2, 1)),
        "w2T": np.ascontiguousarray(ins["L0_ffn_w2"].transpose(0, 2, 1)),
        "w3T": np.ascontiguousarray(ins["L0_ffn_w3"].transpose(0, 2, 1)),
        "b1r": _col_bias(ins["L0_ffn_b1"]),
        "b2r": _col_bias(ins["L0_ffn_b2"]),
        "b3r": _col_bias(ins["L0_ffn_b3"]),
        "sw1T": np.ascontiguousarray(ins["L0_ffn_sw1"].T),
        "sw2T": np.ascontiguousarray(ins["L0_ffn_sw2"].T),
        "sw3T": np.ascontiguousarray(ins["L0_ffn_sw3"].T),
    }
    in_maps = [dict(base, xT=s) for s in shards]
    res = run_bass_kernel_spmd(nc, in_maps, core_ids=list(range(NCORES)))
    return _unshard_T([res.results[c]["yT"] for c in range(NCORES)])





NKV = 4096
HEADS = 16


def build_atta():
    pr = Pr()
    nc = pr.nc
    xTb_d = nc.dram_tensor("xTb", [D, NKV], F32, kind="ExternalInput").ap()
    wqT_d = nc.dram_tensor("wqT", [D, D], F32, kind="ExternalInput").ap()
    wkT_d = nc.dram_tensor("wkT", [D, D], F32, kind="ExternalInput").ap()
    wvT_d = nc.dram_tensor("wvT", [D, D], F32, kind="ExternalInput").ap()
    woT_d = nc.dram_tensor("woT", [D, D], F32, kind="ExternalInput").ap()
    mrow_d = nc.dram_tensor("mrow", [1, NKV], BF16, kind="ExternalInput").ap()
    yT_d = nc.dram_tensor("yT", [D, TOK], F32, kind="ExternalOutput").ap()
    scrk_d = nc.dram_tensor("scrk", [HEADS, 64, NKV], BF16, kind="Internal").ap()

    trim = np.zeros((4, 128, 512), dtype=np.float32)
    for j in range(4):
        for p in range(128):
            trim[j, p, j * 128 + p:] = 1.0
    tri_c = pr.const(trim.reshape(4 * 128, 512), "tri_c")

    xT = pr.sbuf("sxT", [128, KCH * 512], F32)
    xln = pr.sbuf("xln", [128, KCH * 512], BF16)
    wbuf = pr.sbuf("wbuf", [128, KCH * D], BF16)
    vres = pr.sbuf("vres", [128, 32 * D], BF16)
    qtb = pr.sbuf("qtb", [128, KCH * TOK], BF16)
    kst = pr.sbuf("kst", [66, NKV], BF16)
    qst = pr.sbuf("qst", [66, 512], BF16)
    psb = pr.sbuf("psb", [128, 2 * 512], BF16)
    tri = pr.sbuf("tri", [128, 4 * 512], BF16)
    oall = pr.sbuf("oall", [128, KCH * TOK], BF16)
    osb = pr.sbuf("osb", [64, 512], BF16)
    kevac = pr.sbuf("kevac", [128, 512], BF16)
    mcol = pr.sbuf("mcol", [128, 4], F32)
    lrow = pr.sbuf("lrow", [1, 512], F32)
    outsb = pr.sbuf("outsb", [128, 512], F32)
    alloc_ln(pr, "att")
    pr.start()

    ctoks = emit_consts(pr)

    def f_init(E):
        ctoks.append(E.dma(sb3(tri[:, :], 4), tri_c.ap().rearrange("(j p) c -> p j c", p=128), sem="dG"))
    pr.sec("gpsimd", f_init)

    def f_init2(E):
        t1 = E.dma(kst[65:66, :], mrow_d)
        ctoks.append(t1)
    pr.sec("sync", f_init2)

    def f_init3(E):
        E.w(*[c for c in ctoks if c])
        ctoks.append(E.t(E.e.memset(kst[64:65, :], -1.0)))
    pr.sec("gpsimd", f_init3)

    P = pr.ps
    for r in range(NKV // 512):
        _atta_proj_tile(pr, r, xTb_d, wqT_d, wkT_d, wvT_d, scrk_d, ctoks)

    pr.fence()

    for h in range(HEADS):
        _atta_head(pr, h, scrk_d, ctoks)

    pr.fence()

    wo_tok = [None]

    def f_ldwo(E):
        wo_tok[0] = E.dma(sb3(wbuf[:, :], KCH), rearr_kp(woT_d), sem="dG")
    pr.sec("gpsimd", f_ldwo)

    for g in range(4):
        xg = [None]

        def f_ldxg(E, g=g):
            E.w(self_tok(pr, "vector"))
            src = rearr_kp(xTb_d)
            xg[0] = E.dma(sb3(xT[:, :], KCH), src[:, :, TOK + g * 512: TOK + (g + 1) * 512])
        pr.sec("sync", f_ldxg)

        for o in range(KCH):
            om = [None]

            def f_omm(E, g=g, o=o, om=om):
                E.w(wo_tok[0], self_tok(pr, "vector"))
                for k in range(KCH):
                    i = E.e.matmul(P[4 + o % 2][:, :],
                                   wbuf[:, k * D + o * 128: k * D + o * 128 + 128],
                                   pr.sb["oall"][:, k * TOK + g * 512: k * TOK + (g + 1) * 512],
                                   start=(k == 0), stop=(k == KCH - 1))
                om[0] = E.t(i)
            pr.sec("tensor", f_omm)

            def f_oadd(E, g=g, o=o, om=om):
                E.w(om[0], xg[0])
                E.w(("dS", pr.cnt["dS"]))
                t1 = E.t(E.e.tensor_tensor(outsb[:, :], xT[:, o * 512:(o + 1) * 512],
                                           P[4 + o % 2][:, :], ALU.add))
                E.w(t1)
            pr.sec("vector", f_oadd)

            def f_ost(E, g=g, o=o):
                E.w(self_tok(pr, "vector"))
                dst = rearr_kp(yT_d)
                E.dma(dst[:, o, g * 512:(g + 1) * 512], outsb[:, :])
            pr.sec("sync", f_ost)

    return pr.finish()


def _atta_proj_tile(pr, r, xTb_d, wqT_d, wkT_d, wvT_d, scrk_d, ctoks):
    P = pr.ps
    xT = pr.sb["sxT"]; xln = pr.sb["xln"]; wbuf = pr.sb["wbuf"]
    vres = pr.sb["vres"]; qtb = pr.sb["qtb"]; kevac = pr.sb["kevac"]
    pr.fence()
    ld = [None]
    wq = [None]

    def f_ld(E, r=r):
        src = rearr_kp(xTb_d)
        ld[0] = E.dma(sb3(xT[:, :], KCH), src[:, :, r * 512:(r + 1) * 512])
    pr.sec("sync", f_ld)

    def f_ldw(E, r=r):
        wq[0] = E.dma(sb3(wbuf[:, :], KCH), rearr_kp(wkT_d), sem="dG")
    pr.sec("gpsimd", f_ldw)

    ln_toks = emit_ln(pr, xT, xln, 512, [ld[0]] + ctoks, "att")

    for o in range(KCH):
        km = [None]

        def f_kmm(E, r=r, o=o, km=km):
            E.w(wq[0], ln_toks[-1])
            E.w(("dS", pr.cnt["dS"]))
            for k in range(KCH):
                i = E.e.matmul(P[4][:, :],
                               wbuf[:, k * D + o * 128: k * D + o * 128 + 128],
                               xln[:, k * 512:(k + 1) * 512],
                               start=(k == 0), stop=(k == KCH - 1))
            km[0] = E.t(i)
        pr.sec("tensor", f_kmm)

        def f_kev(E, r=r, o=o, km=km):
            E.w(km[0])
            E.w(("dS", pr.cnt["dS"]))
            E.t(E.e.tensor_copy(kevac[:, :], P[4][:, :]))
        pr.sec("vector", f_kev)

        def f_kst(E, r=r, o=o):
            E.w(self_tok(pr, "vector"))
            for hh in range(2):
                h = o * 2 + hh
                E.dma(scrk_d[h, :, r * 512:(r + 1) * 512], kevac[hh * 64:(hh + 1) * 64, :])
        pr.sec("sync", f_kst)

    wv = [None]

    def f_ldwv(E, r=r):
        E.w(self_tok(pr, "tensor"))
        wv[0] = E.dma(sb3(wbuf[:, :], KCH), rearr_kp(wvT_d), sem="dG")
    pr.sec("gpsimd", f_ldwv)

    for ts in range(4):
        for o in range(2):
            vm = [None]

            def f_vmm(E, r=r, ts=ts, o=o, vm=vm):
                E.w(wv[0])
                E.w(self_tok(pr, "vector"))
                for k in range(KCH):
                    i = E.e.matmul(P[4][:, :],
                                   xln[:, k * 512 + ts * 128: k * 512 + (ts + 1) * 128],
                                   wbuf[:, k * D + o * 512: k * D + (o + 1) * 512],
                                   start=(k == 0), stop=(k == KCH - 1))
                vm[0] = E.t(i)
            pr.sec("tensor", f_vmm)

            def f_vev(E, r=r, ts=ts, o=o, vm=vm):
                E.w(vm[0])
                rt = r * 4 + ts
                E.t(E.e.tensor_copy(vres[:, rt * D + o * 512: rt * D + (o + 1) * 512],
                                    P[4][:, :]))
            pr.sec("vector", f_vev)

    if r >= 4:
        wqq = [None]

        def f_ldwq(E, r=r):
            E.w(self_tok(pr, "tensor"))
            wqq[0] = E.dma(sb3(wbuf[:, :], KCH), rearr_kp(wqT_d), sem="dG")
        pr.sec("gpsimd", f_ldwq)

        for o in range(KCH):
            qm = [None]

            def f_qmm(E, r=r, o=o, qm=qm):
                E.w(wqq[0])
                E.w(self_tok(pr, "scalar"))
                for k in range(KCH):
                    i = E.e.matmul(P[5][:, :],
                                   wbuf[:, k * D + o * 128: k * D + o * 128 + 128],
                                   xln[:, k * 512:(k + 1) * 512],
                                   start=(k == 0), stop=(k == KCH - 1))
                qm[0] = E.t(i)
            pr.sec("tensor", f_qmm)

            def f_qev(E, r=r, o=o, qm=qm):
                E.w(qm[0])
                E.t(E.e.activation(qtb[:, o * TOK + (r - 4) * 512: o * TOK + (r - 3) * 512],
                                   P[5][:, :], AF.Copy, scale=0.125))
            pr.sec("scalar", f_qev)


def _atta_head(pr, h, scrk_d, ctoks):
    P = pr.ps
    kst = pr.sb["kst"]; qst = pr.sb["qst"]; qtb = pr.sb["qtb"]
    vres = pr.sb["vres"]; psb = pr.sb["psb"]; tri = pr.sb["tri"]
    oall = pr.sb["oall"]; osb = pr.sb["osb"]; mcol = pr.sb["mcol"]; lrow = pr.sb["lrow"]
    ch, hh = h // 2, h % 2

    kld = [None]

    def f_ldk(E, h=h):
        E.w(self_tok(pr, "tensor"))
        kld[0] = E.dma(kst[0:64, :], scrk_d[h, :, :])
    pr.sec("sync", f_ldk)

    for g in range(4):
        nkt = 20 + 4 * g
        qcols = slice(None)
        q0 = g * 512
        mt = [None] * 4
        stg = [None]

        def f_stgq(E, h=h, g=g, stg=stg):
            E.w(self_tok(pr, "tensor"))
            stg[0] = E.t(E.e.tensor_copy(qst[0:64, :],
                                         qtb[hh * 64:hh * 64 + 64,
                                             ch * TOK + q0:ch * TOK + q0 + 512]))
        pr.sec("vector", f_stgq)

        for qs in range(4):
            red = [None]
            for ks in range(5 + g):
                sm = [None]

                def f_smm(E, h=h, g=g, qs=qs, ks=ks, sm=sm, stg=stg):
                    E.w(kld[0], *ctoks)
                    E.w(stg[0])
                    E.w(self_tok(pr, "vector"))
                    i = E.e.matmul(P[2 + ks % 2][:, :],
                                   qst[0:64, qs * 128:(qs + 1) * 128],
                                   kst[0:64, ks * 512:(ks + 1) * 512],
                                   start=True, stop=True)
                    sm[0] = E.t(i)
                pr.sec("tensor", f_smm)

                def f_red(E, h=h, g=g, qs=qs, ks=ks, sm=sm, red=red):
                    E.w(sm[0])
                    if ks == 0:
                        red[0] = E.t(E.e.reduce_max(mcol[:, qs:qs + 1], P[2 + ks % 2][:, :], axis=AX.X))
                    else:
                        t1 = E.t(E.e.reduce_max(mcol[:, 3:4] if False else pr.sb["att_tmp"][:, 0:1],
                                                P[2 + ks % 2][:, :], axis=AX.X))
                        E.w(t1)
                        red[0] = E.t(E.e.tensor_max(mcol[:, qs:qs + 1], mcol[:, qs:qs + 1],
                                                    pr.sb["att_tmp"][:, 0:1]))
                pr.sec("vector", f_red)
            mt[qs] = red[0]

        def f_mtr(E, h=h, g=g, mt=mt, stg=stg):
            E.w(*mt)
            E.w(self_tok(pr, "scalar"))
            for qs in range(4):
                i = E.e.transpose(P[1][0:1, qs * 128:(qs + 1) * 128],
                                  mcol[:, qs:qs + 1], pr.sb["eye"][:, :])
            stg[0] = E.t(i)
        pr.sec("tensor", f_mtr)

        def f_stg(E, h=h, g=g, stg=stg):
            E.w(stg[0])
            t0 = E.t(E.e.memset(qst[64:66, :], 1.0))
            E.w(t0)
            t2 = E.t(E.e.tensor_copy(qst[64:65, :], P[1][0:1, :]))
            E.w(t2)
            stg[0] = t2
        pr.sec("vector", f_stg)

        ot = [None]
        lt = [None]
        pv_tok = {}
        for kt in range(nkt):
            et = [None]

            def f_stmm(E, h=h, g=g, kt=kt, et=et, stg=stg):
                E.w(stg[0])
                if kt >= 2:
                    E.w(pv_tok[kt - 2])
                i = E.e.matmul(P[4 + kt % 2][:, :],
                               kst[:, kt * 128:(kt + 1) * 128],
                               qst[:, :], start=True, stop=True)
                et[0] = E.t(i)
            pr.sec("tensor", f_stmm)

            def f_exp(E, h=h, g=g, kt=kt, et=et):
                E.w(et[0])
                et[0] = E.t(E.e.activation(psb[:, (kt % 2) * 512:(kt % 2) * 512 + 512],
                                           P[4 + kt % 2][:, :], AF.Exp))
            pr.sec("scalar", f_exp)

            if kt >= nkt - 4:
                j = kt - (nkt - 4)

                def f_tri(E, h=h, g=g, kt=kt, j=j, et=et):
                    E.w(et[0])
                    et[0] = E.t(E.e.tensor_mul(psb[:, (kt % 2) * 512:(kt % 2) * 512 + 512],
                                               psb[:, (kt % 2) * 512:(kt % 2) * 512 + 512],
                                               tri[:, j * 512:(j + 1) * 512]))
                pr.sec("vector", f_tri)

            def f_pv(E, h=h, g=g, kt=kt, et=et):
                E.w(et[0])
                i = E.e.matmul(P[6][0:64, :],
                               vres[:, kt * D + ch * 128 + hh * 64: kt * D + ch * 128 + hh * 64 + 64],
                               psb[:, (kt % 2) * 512:(kt % 2) * 512 + 512],
                               start=(kt == 0), stop=(kt == nkt - 1))
                i2 = E.e.matmul(P[7][0:1, :], pr.sb["ones128b"][:, :],
                                psb[:, (kt % 2) * 512:(kt % 2) * 512 + 512],
                                start=(kt == 0), stop=(kt == nkt - 1))
                pv_tok[kt] = E.t(i2)
            pr.sec("tensor", f_pv)

        nt_ = [None]

        def f_norm_v(E, h=h, g=g, nt_=nt_):
            E.w(pv_tok[nkt - 1])
            t1 = E.t(E.e.reciprocal(lrow[:, :], P[7][0:1, :]))
            E.w(t1)
            t2 = E.t(E.e.tensor_copy(osb[:, :], P[6][0:64, :]))
            nt_[0] = t2
        pr.sec("vector", f_norm_v)

        def f_norm_bc(E, h=h, g=g, nt_=nt_):
            E.w(nt_[0])
            i = E.e.matmul(P[5][0:64, :], pr.sb["ones1r"][:, 0:64], lrow[:, :],
                           start=True, stop=True)
            nt_[0] = E.t(i)
        pr.sec("tensor", f_norm_bc)

        def f_norm_ap(E, h=h, g=g, nt_=nt_):
            E.w(nt_[0])
            E.t(E.e.tensor_mul(oall[hh * 64:hh * 64 + 64,
                                    ch * TOK + q0:ch * TOK + q0 + 512],
                               osb[:, :], P[5][0:64, :]))
        pr.sec("vector", f_norm_ap)


def run_atta(x_flat, ins):
    nc = _get("atta", build_atta)
    in_maps = []
    x = x_flat.reshape(4, NKV, D)
    for c in range(NCORES):
        b, half = c // 2, c % 2
        own = np.ascontiguousarray(x[b, half * TOK:(half + 1) * TOK].T)
        if half == 0:
            pre = np.zeros((D, TOK), dtype=np.float32)
            mrow = np.concatenate([np.full((1, TOK), -1e9, np.float32),
                                   np.zeros((1, TOK), np.float32)], axis=1)
        else:
            pre = np.ascontiguousarray(x[b, 0:TOK].T)
            mrow = np.zeros((1, NKV), np.float32)
        import ml_dtypes
        in_maps.append({
            "xTb": np.concatenate([pre, own], axis=1),
            "wqT": np.ascontiguousarray(ins["L0_attn_a_Wq"].T),
            "wkT": np.ascontiguousarray(ins["L0_attn_a_Wk"].T),
            "wvT": np.ascontiguousarray(ins["L0_attn_a_Wv"].T),
            "woT": np.ascontiguousarray(ins["L0_attn_a_Wo"].T),
            "mrow": mrow.astype(ml_dtypes.bfloat16),
        })
    res = run_bass_kernel_spmd(nc, in_maps, core_ids=list(range(NCORES)))
    return _unshard_T([res.results[c]["yT"] for c in range(NCORES)])



PM = 256


def build_perf():
    pr = Pr()
    nc = pr.nc
    xTb_d = nc.dram_tensor("xTb", [D, NKV], F32, kind="ExternalInput").ap()
    wqT_d = nc.dram_tensor("wqT", [D, D], F32, kind="ExternalInput").ap()
    wkT_d = nc.dram_tensor("wkT", [D, D], F32, kind="ExternalInput").ap()
    wvT_d = nc.dram_tensor("wvT", [D, D], F32, kind="ExternalInput").ap()
    woT_d = nc.dram_tensor("woT", [D, D], F32, kind="ExternalInput").ap()
    projT_d = nc.dram_tensor("projT", [HEADS, 64, PM], F32, kind="ExternalInput").ap()
    bor_d = nc.dram_tensor("bor", [128, KCH], F32, kind="ExternalInput").ap()
    yT_d = nc.dram_tensor("yT", [D, TOK], F32, kind="ExternalOutput").ap()
    scrk_d = nc.dram_tensor("scrk", [HEADS, 64, NKV], BF16, kind="Internal").ap()

    xT = pr.sbuf("sxT", [128, KCH * 512], F32)
    xln = pr.sbuf("xln", [128, KCH * 512], BF16)
    wbuf = pr.sbuf("wbuf", [128, KCH * D], BF16)
    vres = pr.sbuf("vres", [128, 32 * HEADS * 65], BF16)
    qtb = pr.sbuf("qtb", [128, KCH * TOK], BF16)
    kst2 = pr.sbuf("kst2", [64, NKV], BF16)
    qhst = pr.sbuf("qhst", [64, TOK], BF16)
    prj = pr.sbuf("prj", [64, HEADS * PM], BF16)
    sqs = pr.sbuf("sqs", [64, 128], BF16)
    ones64b = pr.sbuf("ones64b", [64, 1], BF16)
    bias4 = pr.sbuf("bias4", [128, 12], F32)
    phik = pr.sbuf("phik", [128, PM], BF16)
    phiqt = pr.sbuf("phiqt", [128, 2 * 128], BF16)
    kvsb = pr.sbuf("kvsb", [128, 2 * 65], BF16)
    oall = pr.sbuf("oall", [128, KCH * TOK], BF16)
    numsb = pr.sbuf("numsb", [64, 128], BF16)
    drow = pr.sbuf("drow", [1, 128], F32)
    outsb = pr.sbuf("outsb", [128, 512], F32)
    bo = pr.sbuf("bo", [128, KCH], F32)
    phifp = pr.sbuf("phifp", [128, PM], F32)
    pr.sbuf("kevb", [128, 512], BF16)
    alloc_ln(pr, "att")
    pr.start()

    ctoks = emit_consts(pr)

    def f_init(E):
        ctoks.append(E.dma(prj[:, :].rearrange("d (h m) -> d h m", h=HEADS),
                           projT_d.rearrange("h d m -> d h m"), sem="dG"))
        ctoks.append(E.t(E.e.memset(ones64b[:, :], 1.0)))
        ctoks.append(E.t(E.e.memset(vres[:, :], 1.0)))
    pr.sec("gpsimd", f_init)

    def f_init2(E):
        ctoks.append(E.dma(bo[:, :], bor_d))
    pr.sec("sync", f_init2)

    P = pr.ps
    for r in range(NKV // 512):
        _perf_proj_tile(pr, r, xTb_d, wqT_d, wkT_d, wvT_d, scrk_d, ctoks)

    pr.fence()

    for h in range(HEADS):
        _perf_head(pr, h, scrk_d, ctoks)

    pr.fence()

    wo_tok = [None]

    def f_ldwo(E):
        wo_tok[0] = E.dma(sb3(wbuf[:, :], KCH), rearr_kp(woT_d), sem="dG")
    pr.sec("gpsimd", f_ldwo)

    for g in range(4):
        xg = [None]

        def f_ldxg(E, g=g):
            E.w(("dS", pr.cnt["dS"]))
            src = rearr_kp(xTb_d)
            xg[0] = E.dma(sb3(xT[:, :], KCH), src[:, :, TOK + g * 512: TOK + (g + 1) * 512])
        pr.sec("sync", f_ldxg)

        for o in range(KCH):
            om = [None]
            ev = [None]

            def f_omm(E, g=g, o=o, om=om):
                E.w(wo_tok[0], self_tok(pr, "vector"))
                for k in range(KCH):
                    i = E.e.matmul(P[4 + o % 2][:, :],
                                   wbuf[:, k * D + o * 128: k * D + o * 128 + 128],
                                   pr.sb["oall"][:, k * TOK + g * 512: k * TOK + (g + 1) * 512],
                                   start=(k == 0), stop=(k == KCH - 1))
                om[0] = E.t(i)
            pr.sec("tensor", f_omm)

            def f_obias(E, g=g, o=o, om=om, ev=ev):
                E.w(om[0])
                E.w(("dS", pr.cnt["dS"]))
                ev[0] = E.t(E.e.activation(outsb[:, :], P[4 + o % 2][:, :], AF.Identity,
                                           bias=bo[:, o:o + 1]))
            pr.sec("scalar", f_obias)

            def f_oadd(E, g=g, o=o, ev=ev):
                E.w(ev[0], xg[0])
                E.t(E.e.tensor_tensor(outsb[:, :], outsb[:, :],
                                      xT[:, o * 512:(o + 1) * 512], ALU.add))
            pr.sec("vector", f_oadd)

            def f_ost(E, g=g, o=o):
                E.w(self_tok(pr, "vector"))
                dst = rearr_kp(yT_d)
                E.dma(dst[:, o, g * 512:(g + 1) * 512], outsb[:, :])
            pr.sec("sync", f_ost)

    return pr.finish()


def _perf_proj_tile(pr, r, xTb_d, wqT_d, wkT_d, wvT_d, scrk_d, ctoks):
    P = pr.ps
    xT = pr.sb["sxT"]; xln = pr.sb["xln"]; wbuf = pr.sb["wbuf"]
    vres = pr.sb["vres"]; qtb = pr.sb["qtb"]
    pr.fence()
    ld = [None]
    wq = [None]

    def f_ld(E, r=r):
        src = rearr_kp(xTb_d)
        ld[0] = E.dma(sb3(xT[:, :], KCH), src[:, :, r * 512:(r + 1) * 512])
    pr.sec("sync", f_ld)

    def f_ldw(E, r=r):
        wq[0] = E.dma(sb3(wbuf[:, :], KCH), rearr_kp(wkT_d), sem="dG")
    pr.sec("gpsimd", f_ldw)

    ln_toks = emit_ln(pr, xT, xln, 512, [ld[0]] + ctoks, "att")
    kevb = pr.sb["kevb"]

    for o in range(KCH):
        km = [None]

        def f_kmm(E, r=r, o=o, km=km):
            E.w(wq[0], ln_toks[-1])
            E.w(("dS", pr.cnt["dS"]))
            for k in range(KCH):
                i = E.e.matmul(P[4][:, :],
                               wbuf[:, k * D + o * 128: k * D + o * 128 + 128],
                               xln[:, k * 512:(k + 1) * 512],
                               start=(k == 0), stop=(k == KCH - 1))
            km[0] = E.t(i)
        pr.sec("tensor", f_kmm)

        def f_kev(E, r=r, o=o, km=km):
            E.w(km[0])
            E.w(("dS", pr.cnt["dS"]))
            E.t(E.e.activation(kevb[:, :], P[4][:, :], AF.Copy, scale=0.125))
        pr.sec("scalar", f_kev)

        def f_kst(E, r=r, o=o):
            E.w(self_tok(pr, "scalar"))
            for hh in range(2):
                h = o * 2 + hh
                E.dma(scrk_d[h, :, r * 512:(r + 1) * 512], kevb[hh * 64:(hh + 1) * 64, :])
        pr.sec("sync", f_kst)

    wv = [None]

    def f_ldwv(E, r=r):
        E.w(self_tok(pr, "tensor"))
        wv[0] = E.dma(sb3(wbuf[:, :], KCH), rearr_kp(wvT_d), sem="dG")
    pr.sec("gpsimd", f_ldwv)

    for ts in range(4):
        for o in range(2):
            vm = [None]

            def f_vmm(E, r=r, ts=ts, o=o, vm=vm):
                E.w(wv[0], *ctoks)
                E.w(self_tok(pr, "vector"))
                for k in range(KCH):
                    i = E.e.matmul(P[4][:, :],
                                   xln[:, k * 512 + ts * 128: k * 512 + (ts + 1) * 128],
                                   wbuf[:, k * D + o * 512: k * D + (o + 1) * 512],
                                   start=(k == 0), stop=(k == KCH - 1))
                vm[0] = E.t(i)
            pr.sec("tensor", f_vmm)

            def f_vev(E, r=r, ts=ts, o=o, vm=vm):
                E.w(vm[0])
                rt = r * 4 + ts
                for hh in range(8):
                    h = o * 8 + hh
                    E.t(E.e.tensor_copy(
                        vres[:, rt * HEADS * 65 + h * 65: rt * HEADS * 65 + h * 65 + 64],
                        P[4][:, hh * 64:(hh + 1) * 64]))
            pr.sec("vector", f_vev)

    if r >= 4:
        wqq = [None]

        def f_ldwq(E, r=r):
            E.w(self_tok(pr, "tensor"))
            wqq[0] = E.dma(sb3(wbuf[:, :], KCH), rearr_kp(wqT_d), sem="dG")
        pr.sec("gpsimd", f_ldwq)

        for o in range(KCH):
            qm = [None]

            def f_qmm(E, r=r, o=o, qm=qm):
                E.w(wqq[0])
                E.w(self_tok(pr, "scalar"))
                for k in range(KCH):
                    i = E.e.matmul(P[5][:, :],
                                   wbuf[:, k * D + o * 128: k * D + o * 128 + 128],
                                   xln[:, k * 512:(k + 1) * 512],
                                   start=(k == 0), stop=(k == KCH - 1))
                qm[0] = E.t(i)
            pr.sec("tensor", f_qmm)

            def f_qev(E, r=r, o=o, qm=qm):
                E.w(qm[0])
                E.t(E.e.activation(qtb[:, o * TOK + (r - 4) * 512: o * TOK + (r - 3) * 512],
                                   P[5][:, :], AF.Copy, scale=0.125))
            pr.sec("scalar", f_qev)


def _perf_phi(pr, h, src_sb, src_off, n_t, bias_idx, dst_fp, dst_bf, tag):
    P = pr.ps
    prj = pr.sb["prj"]; bias4 = pr.sb["bias4"]; sqs = pr.sb["sqs"]
    ones64b = pr.sb["ones64b"]
    t_mm = [None]
    t_bias = [None]

    def f_xw(E):
        E.w(self_tok(pr, "scalar"), self_tok(pr, "vector"))
        i = E.e.matmul(P[4][:, 0:PM], src_sb[0:64, src_off:src_off + 128],
                       prj[:, h * PM:(h + 1) * PM], start=True, stop=True)
        t_mm[0] = E.t(i)
    pr.sec("tensor", f_xw)

    def f_sq(E):
        E.w(self_tok(pr, "tensor"))
        E.t(E.e.activation(sqs[:, 0:128], src_sb[0:64, src_off:src_off + 128], AF.Square))
    pr.sec("scalar", f_sq)

    def f_nq(E):
        E.w(self_tok(pr, "scalar"), self_tok(pr, "vector"))
        i = E.e.matmul(P[5][:, 0:1], sqs[:, 0:128], ones64b[:, :],
                       start=True, stop=True)
        t_mm[0] = E.t(i)
    pr.sec("tensor", f_nq)

    def f_max(E):
        E.w(t_mm[0])
        E.w(self_tok(pr, "scalar"))
        t1 = E.t(E.e.reduce_max(bias4[:, bias_idx:bias_idx + 1], P[4][:, 0:PM], axis=AX.X))
        E.w(t1)
        t2 = E.t(E.e.tensor_copy(bias4[:, bias_idx + 1:bias_idx + 2], P[5][:, 0:1]))
        E.w(t2)
        t3 = E.t(E.e.scalar_tensor_tensor(bias4[:, bias_idx + 2:bias_idx + 3],
                                          bias4[:, bias_idx + 1:bias_idx + 2], -0.5,
                                          bias4[:, bias_idx:bias_idx + 1],
                                          ALU.mult, ALU.subtract))
        t_bias[0] = t3
    pr.sec("vector", f_max)

    def f_exp(E):
        E.w(t_bias[0])
        t_bias[0] = E.t(E.e.activation(dst_fp[:, 0:PM], P[4][:, 0:PM], AF.Exp,
                                       bias=bias4[:, bias_idx + 2:bias_idx + 3]))
    pr.sec("scalar", f_exp)

    def f_eps(E):
        E.w(t_bias[0])
        E.w(self_tok(pr, "tensor"))
        E.t(E.e.tensor_scalar(dst_bf[:, 0:PM], dst_fp[:, 0:PM], 1e-6, None, ALU.add))
    pr.sec("vector", f_eps)


def _perf_head(pr, h, scrk_d, ctoks):
    P = pr.ps
    kst2 = pr.sb["kst2"]; qhst = pr.sb["qhst"]; qtb = pr.sb["qtb"]
    vres = pr.sb["vres"]; phik = pr.sb["phik"]; phifp = pr.sb["phifp"]
    phiqt = pr.sb["phiqt"]; kvsb = pr.sb["kvsb"]; oall = pr.sb["oall"]
    numsb = pr.sb["numsb"]; drow = pr.sb["drow"]
    ch, hh = h // 2, h % 2
    kld = [None]

    def f_ldk(E, h=h):
        E.w(self_tok(pr, "tensor"))
        kld[0] = E.dma(kst2[:, :], scrk_d[h, :, :])
    pr.sec("sync", f_ldk)

    def f_stgq(E, h=h):
        E.w(self_tok(pr, "tensor"), kld[0])
        E.t(E.e.tensor_copy(qhst[:, :], qtb[hh * 64:hh * 64 + 64,
                                            ch * TOK:(ch + 1) * TOK]))
    pr.sec("vector", f_stgq)

    kv_last = [None]
    for rt in range(32):
        _perf_phi(pr, h, kst2, rt * 128, 128, 0, phifp, phik, "k")

        def f_kv(E, h=h, rt=rt):
            E.w(self_tok(pr, "vector"), *ctoks)
            for mc in range(2):
                i = E.e.matmul(P[6 + mc][:, 0:65],
                               phik[:, mc * 128:(mc + 1) * 128],
                               vres[:, rt * HEADS * 65 + h * 65: rt * HEADS * 65 + (h + 1) * 65],
                               start=(rt == 0), stop=(rt == 31))
            kv_last[0] = E.t(i)
        pr.sec("tensor", f_kv)

    def f_kvev(E, h=h):
        E.w(kv_last[0])
        t1 = E.t(E.e.tensor_copy(kvsb[:, 0:65], P[6][:, 0:65]))
        t2 = E.t(E.e.tensor_copy(kvsb[:, 65:130], P[7][:, 0:65]))
    pr.sec("vector", f_kvev)

    for rt in range(16):
        _perf_phi(pr, h, qhst, rt * 128, 128, 4, phifp, phik, "q")

        tt = [None]

        def f_tr(E, h=h, rt=rt, tt=tt):
            E.w(self_tok(pr, "vector"))
            for mc in range(2):
                i = E.e.transpose(P[5].ap().bitcast(BF16)[0:128, mc * 128:(mc + 1) * 128],
                                  phik[:, mc * 128:(mc + 1) * 128],
                                  pr.sb["eyeb"][:, :])
            tt[0] = E.t(i)
        pr.sec("tensor", f_tr)

        def f_trc(E, h=h, rt=rt, tt=tt):
            E.w(tt[0])
            tt[0] = E.t(E.e.tensor_copy(phiqt[:, :], P[5].ap().bitcast(BF16)[0:128, 0:256]))
        pr.sec("vector", f_trc)

        nd = [None]

        def f_num(E, h=h, rt=rt, tt=tt, nd=nd):
            E.w(tt[0])
            for mc in range(2):
                i = E.e.matmul(P[4][0:65, 0:128], kvsb[:, mc * 65:(mc + 1) * 65],
                               phiqt[:, mc * 128:(mc + 1) * 128],
                               start=(mc == 0), stop=(mc == 1))
            nd[0] = E.t(i)
        pr.sec("tensor", f_num)

        def f_den(E, h=h, rt=rt, nd=nd):
            E.w(nd[0])
            t1 = E.t(E.e.tensor_scalar(drow[:, :], P[4][64:65, 0:128], 1e-6, None, ALU.add))
            E.w(t1)
            t2 = E.t(E.e.reciprocal(drow[:, :], drow[:, :]))
            E.w(t2)
            t3 = E.t(E.e.tensor_copy(numsb[:, :], P[4][0:64, 0:128]))
            nd[0] = t3
        pr.sec("vector", f_den)

        def f_dbc(E, h=h, rt=rt, nd=nd):
            E.w(nd[0])
            i = E.e.matmul(P[5][0:64, 0:128], pr.sb["ones1r"][:, 0:64], drow[:, :],
                           start=True, stop=True)
            nd[0] = E.t(i)
        pr.sec("tensor", f_dbc)

        def f_oap(E, h=h, rt=rt, nd=nd):
            E.w(nd[0])
            E.t(E.e.tensor_mul(oall[hh * 64:hh * 64 + 64,
                                    ch * TOK + rt * 128: ch * TOK + (rt + 1) * 128],
                               numsb[:, :], P[5][0:64, 0:128]))
        pr.sec("vector", f_oap)


def run_perf(x_flat, ins):
    nc = _get("perf", build_perf)
    in_maps = []
    x = x_flat.reshape(4, NKV, D)
    for c in range(NCORES):
        b, half = c // 2, c % 2
        own = np.ascontiguousarray(x[b, half * TOK:(half + 1) * TOK].T)
        pre = np.ascontiguousarray(x[b, (1 - half) * TOK:(2 - half) * TOK].T)
        bor = np.ascontiguousarray(
            ins["L0_attn_b_bo"].reshape(8, 128).T)
        in_maps.append({
            "xTb": np.concatenate([pre, own], axis=1),
            "wqT": np.ascontiguousarray(ins["L0_attn_b_Wq"].T),
            "wkT": np.ascontiguousarray(ins["L0_attn_b_Wk"].T),
            "wvT": np.ascontiguousarray(ins["L0_attn_b_Wv"].T),
            "woT": np.ascontiguousarray(ins["L0_attn_b_Wo"].T),
            "projT": np.ascontiguousarray(ins["L0_attn_b_proj"].transpose(0, 2, 1)),
            "bor": bor,
        })
    res = run_bass_kernel_spmd(nc, in_maps, core_ids=list(range(NCORES)))
    return _unshard_T([res.results[c]["yT"] for c in range(NCORES)])



DI = 512
GH = 8
SB16 = 16


def build_gla_qkva():
    pr = Pr()
    nc = pr.nc
    xT_d = nc.dram_tensor("xT", [D, TOK], F32, kind="ExternalInput").ap()
    wqn_d = nc.dram_tensor("wqn", [D, DI], F32, kind="ExternalInput").ap()
    wkn_d = nc.dram_tensor("wkn", [D, DI], F32, kind="ExternalInput").ap()
    wvn_d = nc.dram_tensor("wvn", [D, DI], F32, kind="ExternalInput").ap()
    wan_d = nc.dram_tensor("wan", [D, DI], F32, kind="ExternalInput").ap()
    bar_d = nc.dram_tensor("bar", [128, 4], F32, kind="ExternalInput").ap()
    gr_d = nc.dram_tensor("gr", [128, KCH], F32, kind="ExternalInput").ap()
    br_d = nc.dram_tensor("br", [128, KCH], F32, kind="ExternalInput").ap()
    res_d = nc.dram_tensor("resT", [D, TOK], BF16, kind="ExternalOutput").ap()
    qT_d = nc.dram_tensor("qTs", [GH, 64, TOK], BF16, kind="ExternalOutput").ap()
    kT_d = nc.dram_tensor("kTs", [GH, 64, TOK], BF16, kind="ExternalOutput").ap()
    la_d = nc.dram_tensor("las", [GH, 64, TOK], F32, kind="ExternalOutput").ap()
    kt_d = nc.dram_tensor("kts", [GH, TOK, 64], BF16, kind="ExternalOutput").ap()
    vt_d = nc.dram_tensor("vts", [GH, TOK, 64], BF16, kind="ExternalOutput").ap()

    xT = pr.sbuf("sxT", [128, KCH * 512], F32)
    ln1 = pr.sbuf("ln1", [128, KCH * 512], F32)
    ln1b = pr.sbuf("ln1b", [128, KCH * 512], BF16)
    xn = pr.sbuf("xn", [128, KCH * 512], BF16)
    wbuf = pr.sbuf("wbuf", [128, KCH * DI], BF16)
    gam = pr.sbuf("gam", [128, KCH], F32)
    bet = pr.sbuf("bet", [128, KCH], F32)
    bacol = pr.sbuf("bacol", [128, 4], F32)
    qf = pr.sbuf("qf", [128, 4 * 512], BF16)
    kf = pr.sbuf("kf", [128, 4 * 512], BF16)
    vf = pr.sbuf("vf", [128, 4 * 512], BF16)
    af = pr.sbuf("af", [128, 4 * 512], F32)
    ksq = pr.sbuf("ksq", [128, 512], BF16)
    nrow = pr.sbuf("nrow", [1, 2 * 512], F32)
    ktn = pr.sbuf("ktn", [128, 512], BF16)
    alloc_ln(pr, "att")
    pr.start()

    ctoks = emit_consts(pr)

    def f_i2(E):
        ctoks.append(E.dma(gam[:, :], gr_d))
        ctoks.append(E.dma(bet[:, :], br_d))
        ctoks.append(E.dma(bacol[:, :], bar_d))
    pr.sec("sync", f_i2)

    P = pr.ps
    for r in range(TOK // 512):
        _glaq_tile(pr, r, xT_d, wqn_d, wkn_d, wvn_d, wan_d,
                   res_d, qT_d, kT_d, la_d, kt_d, vt_d, ctoks)
    return pr.finish()


def _glaq_tile(pr, r, xT_d, wqn_d, wkn_d, wvn_d, wan_d,
               res_d, qT_d, kT_d, la_d, kt_d, vt_d, ctoks):
    P = pr.ps
    xT = pr.sb["sxT"]; ln1 = pr.sb["ln1"]; ln1b = pr.sb["ln1b"]; xn = pr.sb["xn"]
    wbuf = pr.sb["wbuf"]; qf = pr.sb["qf"]; kf = pr.sb["kf"]; vf = pr.sb["vf"]
    af = pr.sb["af"]; ksq = pr.sb["ksq"]; nrow = pr.sb["nrow"]; ktn = pr.sb["ktn"]
    pr.fence()
    ld = [None]

    def f_ld(E, r=r):
        src = rearr_kp(xT_d)
        ld[0] = E.dma(sb3(xT[:, :], KCH), src[:, :, r * 512:(r + 1) * 512])
    pr.sec("sync", f_ld)

    l1 = emit_ln(pr, xT, ln1, 512, [ld[0]] + ctoks, "att")
    def f_cp(E, r=r):
        E.w(l1[-1])
        for k in range(KCH):
            E.t(E.e.tensor_copy(ln1b[:, k * 512:(k + 1) * 512], ln1[:, k * 512:(k + 1) * 512]))
    pr.sec("gpsimd", f_cp)

    def f_str(E, r=r):
        E.w(self_tok(pr, "gpsimd"))
        dst = rearr_kp(res_d)
        E.dma(dst[:, :, r * 512:(r + 1) * 512], sb3(ln1b[:, :], KCH))
    pr.sec("sync", f_str)

    l2 = emit_ln(pr, ln1, xn, 512, [l1[-1]], "att",
                 gamma_beta=(pr.sb["gam"], pr.sb["bet"]))

    for (wd, dstb, isa) in ((wqn_d, qf, 0), (wkn_d, kf, 0), (wvn_d, vf, 0), (wan_d, af, 1)):
        wt = [None]

        def f_ldw(E, wd=wd):
            E.w(self_tok(pr, "tensor"))
            wt[0] = E.dma(sb3(wbuf[:, 0:KCH * DI], KCH),
                          wd.rearrange("(k p) n -> p k n", p=128), sem="dG")
        pr.sec("gpsimd", f_ldw)

        for o in range(4):
            pm = [None]

            def f_mm(E, r=r, o=o, wt=wt, pm=pm):
                E.w(wt[0], l2[-1])
                E.w(self_tok(pr, "scalar"), self_tok(pr, "vector"))
                for k in range(KCH):
                    i = E.e.matmul(P[4 + o % 2][:, :],
                                   wbuf[:, k * DI + o * 128: k * DI + o * 128 + 128],
                                   xn[:, k * 512:(k + 1) * 512],
                                   start=(k == 0), stop=(k == KCH - 1))
                pm[0] = E.t(i)
            pr.sec("tensor", f_mm)

            if isa == 0:
                def f_ev(E, o=o, pm=pm, dstb=dstb):
                    E.w(pm[0])
                    E.t(E.e.activation(dstb[:, o * 512:(o + 1) * 512],
                                       P[4 + o % 2][:, :], AF.Copy))
                pr.sec("scalar", f_ev)
            else:
                sg = [None]

                def f_sig(E, o=o, pm=pm, sg=sg):
                    E.w(pm[0])
                    t1 = E.t(E.e.activation(pr.sb["att_tmp"][:, :], P[4 + o % 2][:, :],
                                            AF.Sigmoid, bias=pr.sb["bacol"][:, o:o + 1]))
                    E.w(t1)
                    sg[0] = E.t(E.e.activation(af[:, o * 512:(o + 1) * 512],
                                               pr.sb["att_tmp"][:, :], AF.Ln))
                pr.sec("scalar", f_sig)

                def f_cl(E, o=o, sg=sg):
                    E.w(sg[0])
                    E.t(E.e.tensor_scalar_max(af[:, o * 512:(o + 1) * 512],
                                              af[:, o * 512:(o + 1) * 512], -5.0))
                pr.sec("vector", f_cl)

    for o in range(4):
        sqt = [None]
        nr = [None]

        def f_ksq(E, r=r, o=o, sqt=sqt):
            E.w(self_tok(pr, "tensor"), self_tok(pr, "vector"))
            sqt[0] = E.t(E.e.activation(ksq[:, :], kf[:, o * 512:(o + 1) * 512], AF.Square))
        pr.sec("scalar", f_ksq)

        def f_nmm(E, o=o, sqt=sqt, nr=nr):
            E.w(sqt[0])
            E.w(self_tok(pr, "vector"))
            i = E.e.matmul(P[0][0:1, :], pr.sb["ones128b"][0:64, :], ksq[0:64, :],
                           start=True, stop=True)
            i = E.e.matmul(P[1][0:1, :], pr.sb["ones128b"][64:128, :], ksq[64:128, :],
                           start=True, stop=True)
            nr[0] = E.t(i)
        pr.sec("tensor", f_nmm)

        def f_nr(E, o=o, nr=nr):
            E.w(nr[0])
            t1 = E.t(E.e.tensor_copy(nrow[:, 0:512], P[0][0:1, :]))
            t2 = E.t(E.e.tensor_copy(nrow[:, 512:1024], P[1][0:1, :]))
            E.w(t1, t2)
            nr[0] = t2
        pr.sec("vector", f_nr)

        def f_ns(E, o=o, nr=nr):
            E.w(nr[0])
            nr[0] = E.t(E.e.activation(nrow[:, :], nrow[:, :], AF.Sqrt))
        pr.sec("scalar", f_ns)

        def f_nrec(E, o=o, nr=nr):
            E.w(nr[0])
            t1 = E.t(E.e.tensor_scalar_max(nrow[:, :], nrow[:, :], 1e-12))
            E.w(t1)
            t2 = E.t(E.e.reciprocal(nrow[:, 0:512], nrow[:, 0:512]))
            E.w(t2)
            t3 = E.t(E.e.reciprocal(nrow[:, 512:1024], nrow[:, 512:1024]))
            nr[0] = t3
        pr.sec("vector", f_nrec)

        def f_nbc(E, o=o, nr=nr):
            E.w(nr[0])
            E.e.matmul(P[2][0:64, :], pr.sb["ones1r"][:, 0:64], nrow[:, 0:512],
                       start=True, stop=True)
            i = E.e.matmul(P[2][64:128, :], pr.sb["ones1r"][:, 0:64], nrow[:, 512:1024],
                           start=True, stop=True)
            nr[0] = E.t(i)
        pr.sec("tensor", f_nbc)

        def f_nap(E, o=o, nr=nr):
            E.w(nr[0])
            E.t(E.e.tensor_mul(kf[:, o * 512:(o + 1) * 512],
                               kf[:, o * 512:(o + 1) * 512], P[2][:, :]))
        pr.sec("vector", f_nap)

    pr.fence()

    def f_stq(E, r=r):
        for o in range(4):
            for hh in range(2):
                h = o * 2 + hh
                E.dma(qT_d[h, :, r * 512:(r + 1) * 512],
                      qf[hh * 64:(hh + 1) * 64, o * 512:(o + 1) * 512])
                E.dma(kT_d[h, :, r * 512:(r + 1) * 512],
                      kf[hh * 64:(hh + 1) * 64, o * 512:(o + 1) * 512])
                E.dma(la_d[h, :, r * 512:(r + 1) * 512],
                      af[hh * 64:(hh + 1) * 64, o * 512:(o + 1) * 512])
    pr.sec("sync", f_stq)

    for src_b, dst_d in ((pr.sb["kf"], kt_d), (pr.sb["vf"], vt_d)):
        for o in range(4):
            for ts in range(4):
                tt = [None]

                def f_tr(E, o=o, ts=ts, src_b=src_b, tt=tt):
                    E.w(self_tok(pr, "vector"), self_tok(pr, "sync"))
                    E.w(("dS", pr.cnt["dS"]))
                    i = E.e.transpose(P[3].ap().bitcast(BF16)[0:128, 0:128],
                                      src_b[:, o * 512 + ts * 128: o * 512 + (ts + 1) * 128],
                                      pr.sb["eyeb"][:, :])
                    tt[0] = E.t(i)
                pr.sec("tensor", f_tr)

                def f_trc(E, o=o, ts=ts, tt=tt):
                    E.w(tt[0])
                    tt[0] = E.t(E.e.tensor_copy(ktn[:, 0:128],
                                                P[3].ap().bitcast(BF16)[0:128, 0:128]))
                pr.sec("vector", f_trc)

                def f_st2(E, r=r, o=o, ts=ts, dst_d=dst_d, tt=tt):
                    E.w(tt[0])
                    for hh in range(2):
                        h = o * 2 + hh
                        E.dma(dst_d[h, r * 512 + ts * 128: r * 512 + (ts + 1) * 128, :],
                              ktn[:, hh * 64:(hh + 1) * 64])
                pr.sec("sync", f_st2)


def build_gla_scan():
    pr = Pr()
    nc = pr.nc
    qT_d = nc.dram_tensor("qTs", [4, 64, NKV], BF16, kind="ExternalInput").ap()
    kT_d = nc.dram_tensor("kTs", [4, 64, NKV], BF16, kind="ExternalInput").ap()
    la_d = nc.dram_tensor("las", [4, 64, NKV], F32, kind="ExternalInput").ap()
    kt_d = nc.dram_tensor("kts", [4, NKV, 64], BF16, kind="ExternalInput").ap()
    vt_d = nc.dram_tensor("vts", [4, NKV, 64], BF16, kind="ExternalInput").ap()
    o_d = nc.dram_tensor("os", [4, 64, NKV], BF16, kind="ExternalOutput").ap()

    bd = np.zeros((128, 128), dtype=np.float32)
    for s in range(128):
        sub = s // 16
        bd[s, s:(sub + 1) * 16] = 1.0
    tri_c = pr.const(bd, "bd128_c")

    qT4 = pr.sbuf("qT4", [64, 4 * NKV], BF16)
    kT4 = pr.sbuf("kT4", [64, 4 * NKV], BF16)
    kt4 = pr.sbuf("kt4", [128, 4 * 32 * 64], BF16)
    vt4 = pr.sbuf("vt4", [128, 4 * 32 * 64], BF16)
    oacc = pr.sbuf("oacc", [64, 4 * NKV], BF16)
    lab = pr.sbuf("lab", [64, 2 * 4 * 128], F32)
    LS = pr.sbuf("LS", [64, 4 * 128], F32)
    nLS = pr.sbuf("nLS", [64, 4 * 128], F32)
    reb = pr.sbuf("reb", [64, 4 * 128], F32)
    dif = pr.sbuf("dif", [64, 4 * 128], F32)
    BT = pr.sbuf("BT", [64, 4 * 128], F32)
    EIV = pr.sbuf("EIV", [64, 4 * 128], F32)
    qb4 = pr.sbuf("qb4", [64, 4 * 128], BF16)
    kiv4 = pr.sbuf("kiv4", [64, 4 * 128], BF16)
    eivt = pr.sbuf("eivt", [128, 4 * 64], BF16)
    kivt = pr.sbuf("kivt", [128, 4 * 64], BF16)
    S4 = pr.sbuf("S4", [64, 4 * 64], BF16)
    St = pr.sbuf("St", [64, 4 * 64], BF16)
    PTm = pr.sbuf("PTm", [128, 128], BF16)
    tri = pr.sbuf("tri16", [128, 128], BF16)
    zc = pr.sbuf("zc", [64, 1], F32)
    subm = pr.sbuf("subm", [128, 8], F32)
    kms = pr.sbuf("kms", [128, 64], BF16)
    pr.start()

    ctoks = emit_consts(pr)

    sm = np.zeros((128, 8), dtype=np.float32)
    for s in range(8):
        sm[s * 16:(s + 1) * 16, s] = 1.0
    sm_c = pr.const(sm, "subm_c")

    def f_i(E):
        ctoks.append(E.dma(tri[:, :], tri_c.ap(), sem="dG"))
        ctoks.append(E.dma(subm[:, :], sm_c.ap(), sem="dG"))
        ctoks.append(E.t(E.e.memset(zc[:, :], 0.0)))
        ctoks.append(E.t(E.e.memset(pr.sb["S4"][:, :], 0.0)))
    pr.sec("gpsimd", f_i)

    ldt = [None]

    def f_ld0(E):
        for u in range(4):
            E.dma(qT4[:, u * NKV:(u + 1) * NKV], qT_d[u, :, :])
            E.dma(kT4[:, u * NKV:(u + 1) * NKV], kT_d[u, :, :])
            E.dma(sb3(kt4[:, u * 2048:(u + 1) * 2048], 32),
                  kt_d[u, :, :].rearrange("(r p) d -> p r d", p=128))
            ldt[0] = E.dma(sb3(vt4[:, u * 2048:(u + 1) * 2048], 32),
                           vt_d[u, :, :].rearrange("(r p) d -> p r d", p=128))
    pr.sec("sync", f_ld0)
    ctoks.append(ldt[0])

    P = pr.ps
    la_tok = [None, None]

    def f_lal(E, j=0):
        for u in range(4):
            la_tok[0] = E.dma(lab[:, u * 128:(u + 1) * 128], la_d[u, :, 0:128])
    pr.sec("sync", f_lal)

    for j in range(32):
        _gla_block(pr, j, la_d, la_tok, ctoks)

    pr.fence()

    def f_st(E):
        for u in range(4):
            E.dma(o_d[u, :, :], oacc[:, u * NKV:(u + 1) * NKV])
    pr.sec("sync", f_st)
    return pr.finish()


def _gla_block(pr, j, la_d, la_tok, ctoks):
    P = pr.ps
    sb = pr.sb
    buf = j % 2
    nbuf = (j + 1) % 2
    lab = sb["lab"]; LS = sb["LS"]; nLS = sb["nLS"]; reb = sb["reb"]; dif = sb["dif"]
    BT = sb["BT"]; EIV = sb["EIV"]; qb4 = sb["qb4"]; kiv4 = sb["kiv4"]
    eivt = sb["eivt"]; kivt = sb["kivt"]; S4 = sb["S4"]; St = sb["St"]
    PTm = sb["PTm"]; tri = sb["tri16"]; zc = sb["zc"]; subm = sb["subm"]; kms = sb["kms"]
    qT4 = sb["qT4"]; kT4 = sb["kT4"]; kt4 = sb["kt4"]; vt4 = sb["vt4"]; oacc = sb["oacc"]
    lslice = lambda u: slice(buf * 512 + u * 128, buf * 512 + (u + 1) * 128)

    if j < 31:
        def f_lal(E, j=j):
            E.w(self_tok(pr, "vector"))
            for u in range(4):
                la_tok[0] = E.dma(lab[:, nbuf * 512 + u * 128: nbuf * 512 + (u + 1) * 128],
                                  la_d[u, :, (j + 1) * 128:(j + 2) * 128])
        pr.sec("sync", f_lal)

    cu = [None]

    def f_scan(E, j=j, cu=cu):
        E.w(("dS", pr.cnt["dS"]), *ctoks)
        t = None
        for u in range(4):
            t = E.t(E.e.tensor_tensor_scan(LS[:, u * 128:(u + 1) * 128],
                                           lab[:, lslice(u)], lab[:, lslice(u)],
                                           0.0, ALU.add, ALU.bypass))
        E.w(t)
        tz = None
        for u in range(4):
            for s in range(8):
                srccol = zc[:, 0:1] if s == 0 else LS[:, u * 128 + s * 16 - 1: u * 128 + s * 16]
                tz = E.t(E.e.tensor_copy(
                    reb[:, u * 128 + s * 16: u * 128 + (s + 1) * 16],
                    srccol.to_broadcast([64, 16])))
        E.w(tz)
        t3 = E.t(E.e.tensor_sub(dif[:, :], LS[:, :], reb[:, :]))
        E.w(t3)
        cu[0] = t3
    pr.sec("vector", f_scan)

    ex = [None, None]

    def f_exp(E, j=j, cu=cu, ex=ex):
        E.w(cu[0])
        ex[0] = E.t(E.e.activation(BT[:, :], dif[:, :], AF.Exp))
        ex[1] = E.t(E.e.activation(EIV[:, :], dif[:, :], AF.Exp, scale=-1.0))
    pr.sec("scalar", f_exp)

    mu = [None]

    def f_mul2(E, j=j, ex=ex, mu=mu):
        E.w(ex[0], ex[1])
        E.w(self_tok(pr, "tensor"))
        t = None
        for u in range(4):
            t = E.t(E.e.tensor_mul(qb4[:, u * 128:(u + 1) * 128],
                                   qT4[:, u * NKV + j * 128: u * NKV + (j + 1) * 128],
                                   BT[:, u * 128:(u + 1) * 128]))
            t = E.t(E.e.tensor_mul(kiv4[:, u * 128:(u + 1) * 128],
                                   kT4[:, u * NKV + j * 128: u * NKV + (j + 1) * 128],
                                   EIV[:, u * 128:(u + 1) * 128]))
        mu[0] = t
    pr.sec("vector", f_mul2)

    ett = [None]

    def f_etr(E, j=j, mu=mu, ett=ett):
        E.w(mu[0])
        for u in range(4):
            i = E.e.transpose(P[0][:, u * 64:(u + 1) * 64],
                              EIV[:, u * 128:(u + 1) * 128],
                              pr.sb["eye"][0:64, 0:64])
        ett[0] = E.t(i)
    pr.sec("tensor", f_etr)

    def f_etc(E, j=j, ett=ett):
        E.w(ett[0])
        t = None
        for u in range(4):
            t = E.t(E.e.tensor_copy(eivt[:, u * 64:(u + 1) * 64],
                                    P[0][:, u * 64:(u + 1) * 64]))
        E.w(t)
        t2 = None
        for u in range(4):
            t2 = E.t(E.e.tensor_mul(kivt[:, u * 64:(u + 1) * 64],
                                    kt4[:, u * 2048 + j * 64: u * 2048 + (j + 1) * 64],
                                    eivt[:, u * 64:(u + 1) * 64]))
        ett[0] = t2
    pr.sec("vector", f_etc)

    for u in range(4):
        pt = [None]

        def f_pt(E, j=j, u=u, pt=pt, ett=ett):
            E.w(ett[0], self_tok(pr, "vector"))
            i = E.e.matmul(P[1][:, 0:128],
                           kiv4[:, u * 128:(u + 1) * 128],
                           qb4[:, u * 128:(u + 1) * 128],
                           start=True, stop=True)
            pt[0] = E.t(i)
        pr.sec("tensor", f_pt)

        def f_ptm(E, j=j, u=u, pt=pt):
            E.w(pt[0])
            pt[0] = E.t(E.e.tensor_mul(PTm[:, :], P[1][:, 0:128], tri[:, :]))
        pr.sec("vector", f_ptm)

        pv = [None]

        def f_pv(E, j=j, u=u, pt=pt, pv=pv):
            E.w(pt[0])
            i = E.e.matmul(P[2][0:64, u * 128:(u + 1) * 128],
                           vt4[:, u * 2048 + j * 64: u * 2048 + (j + 1) * 64],
                           PTm[:, :], start=True, stop=False, skip_group_check=True)
            pv[0] = E.t(i)
        pr.sec("tensor", f_pv)

        for s in range(8):
            sup = [None]

            def f_int(E, j=j, u=u, s=s, pv=pv):
                E.w(self_tok(pr, "vector"))
                E.e.matmul(P[2][0:64, u * 128 + s * 16: u * 128 + (s + 1) * 16],
                           S4[:, u * 64:(u + 1) * 64],
                           qb4[:, u * 128 + s * 16: u * 128 + (s + 1) * 16],
                           start=False, stop=(s == 7), skip_group_check=True)
            pr.sec("tensor", f_int)

            def f_kms(E, j=j, u=u, s=s, sup=sup):
                E.w(self_tok(pr, "tensor"))
                sup[0] = E.t(E.e.tensor_scalar_mul(kms[:, :],
                                                   kivt[:, u * 64:(u + 1) * 64],
                                                   subm[:, s:s + 1]))
            pr.sec("vector", f_kms)

            um = [None]

            def f_umm(E, j=j, u=u, s=s, sup=sup, um=um):
                E.w(sup[0])
                i = E.e.matmul(P[3][0:64, 0:64], kms[:, :],
                               vt4[:, u * 2048 + j * 64: u * 2048 + (j + 1) * 64],
                               start=True, stop=True)
                um[0] = E.t(i)
            pr.sec("tensor", f_umm)

            def f_sup(E, j=j, u=u, s=s, um=um):
                E.w(um[0])
                t1 = E.t(E.e.tensor_add(St[:, u * 64:(u + 1) * 64],
                                        S4[:, u * 64:(u + 1) * 64], P[3][0:64, 0:64]))
                E.w(t1)
                E.t(E.e.tensor_scalar_mul(
                    S4[:, u * 64:(u + 1) * 64], St[:, u * 64:(u + 1) * 64],
                    BT[:, u * 128 + (s + 1) * 16 - 1: u * 128 + (s + 1) * 16]))
            pr.sec("vector", f_sup)

        def f_oev(E, j=j, u=u):
            E.w(self_tok(pr, "tensor"))
            E.t(E.e.tensor_copy(oacc[:, u * NKV + j * 128: u * NKV + (j + 1) * 128],
                                P[2][0:64, u * 128:(u + 1) * 128]))
        pr.sec("vector", f_oev)



def build_gla_out():
    pr = Pr()
    nc = pr.nc
    xT_d = nc.dram_tensor("xT", [D, TOK], F32, kind="ExternalInput").ap()
    res_d = nc.dram_tensor("resT", [D, TOK], BF16, kind="ExternalInput").ap()
    o_d = nc.dram_tensor("os", [GH, 64, TOK], BF16, kind="ExternalInput").ap()
    won_d = nc.dram_tensor("won", [DI, D], F32, kind="ExternalInput").ap()
    yT_d = nc.dram_tensor("yT", [D, TOK], F32, kind="ExternalOutput").ap()

    xT = pr.sbuf("sxT", [128, KCH * 512], F32)
    resb = pr.sbuf("resb", [128, KCH * 512], BF16)
    ot = pr.sbuf("ot", [128, 4 * TOK], BF16)
    wbuf = pr.sbuf("wbuf", [128, 4 * D], BF16)
    outsb = pr.sbuf("outsb", [128, 512], F32)
    pr.start()
    ctoks = emit_consts(pr)
    P = pr.ps

    wt = [None]
    olt = [None]

    def f_ldw(E):
        wt[0] = E.dma(sb3(wbuf[:, :], 4), won_d.rearrange("(k p) n -> p k n", p=128), sem="dG")
    pr.sec("gpsimd", f_ldw)

    def f_ldo(E):
        olt[0] = E.dma(ot[:, :].rearrange("p (k t) -> p k t", k=4),
                       o_d.rearrange("(k hh) d t -> (hh d) k t", k=4))
    pr.sec("sync", f_ldo)

    for g in range(4):
        xg = [None]

        def f_ldx(E, g=g):
            E.w(("dS", pr.cnt["dS"]))
            E.dma(sb3(xT[:, :], KCH), rearr_kp(xT_d)[:, :, g * 512:(g + 1) * 512])
            xg[0] = E.dma(sb3(resb[:, :], KCH), rearr_kp(res_d)[:, :, g * 512:(g + 1) * 512])
        pr.sec("sync", f_ldx)

        for o in range(KCH):
            om = [None]

            def f_omm(E, g=g, o=o, om=om):
                E.w(wt[0], olt[0], self_tok(pr, "vector"))
                for k in range(4):
                    i = E.e.matmul(P[4 + o % 2][:, :],
                                   wbuf[:, k * D + o * 128: k * D + o * 128 + 128],
                                   ot[:, k * TOK + g * 512: k * TOK + (g + 1) * 512],
                                   start=(k == 0), stop=(k == 3))
                om[0] = E.t(i)
            pr.sec("tensor", f_omm)

            def f_oadd(E, g=g, o=o, om=om):
                E.w(om[0], xg[0])
                E.w(("dS", pr.cnt["dS"]))
                t1 = E.t(E.e.tensor_tensor(outsb[:, :], xT[:, o * 512:(o + 1) * 512],
                                           P[4 + o % 2][:, :], ALU.add))
                E.w(t1)
                E.t(E.e.tensor_tensor(outsb[:, :], outsb[:, :],
                                      resb[:, o * 512:(o + 1) * 512], ALU.add))
            pr.sec("vector", f_oadd)

            def f_ost(E, g=g, o=o):
                E.w(self_tok(pr, "vector"))
                E.dma(rearr_kp(yT_d)[:, o, g * 512:(g + 1) * 512], outsb[:, :])
            pr.sec("sync", f_ost)

    return pr.finish()


def _colchunks(v):
    return np.ascontiguousarray(v.reshape(8, 128).T)


def run_gla(x_flat, ins):
    import ml_dtypes
    nc1 = _get("gla_qkva", build_gla_qkva)
    shards = _shard_T(x_flat)
    ba = ins["L1_attn_a_ba"]
    base = {
        "wqn": np.ascontiguousarray(ins["L1_attn_a_Wq"]),
        "wkn": np.ascontiguousarray(ins["L1_attn_a_Wk"]),
        "wvn": np.ascontiguousarray(ins["L1_attn_a_Wv"]),
        "wan": np.ascontiguousarray(ins["L1_attn_a_Wa"]),
        "bar": np.ascontiguousarray(ba.reshape(4, 128).T),
        "gr": _colchunks(ins["L1_attn_a_gamma"]),
        "br": _colchunks(ins["L1_attn_a_beta"]),
    }
    res = run_bass_kernel_spmd(nc1, [dict(base, xT=s) for s in shards],
                               core_ids=list(range(NCORES)))
    bf = ml_dtypes.bfloat16
    qs = np.zeros((4, 8, 64, NKV), dtype=bf)
    ks = np.zeros((4, 8, 64, NKV), dtype=bf)
    las = np.zeros((4, 8, 64, NKV), dtype=np.float32)
    kts = np.zeros((4, 8, NKV, 64), dtype=bf)
    vts = np.zeros((4, 8, NKV, 64), dtype=bf)
    resT = []
    for c in range(NCORES):
        b, half = c // 2, c % 2
        r = res.results[c]
        sl = slice(half * TOK, (half + 1) * TOK)
        qs[b, :, :, sl] = r["qTs"]
        ks[b, :, :, sl] = r["kTs"]
        las[b, :, :, sl] = r["las"]
        kts[b, :, sl, :] = r["kts"]
        vts[b, :, sl, :] = r["vts"]
        resT.append(r["resT"])

    nc2 = _get("gla_scan", build_gla_scan)
    qsf = qs.reshape(32, 64, NKV)
    ksf = ks.reshape(32, 64, NKV)
    lasf = las.reshape(32, 64, NKV)
    ktsf = kts.reshape(32, NKV, 64)
    vtsf = vts.reshape(32, NKV, 64)
    in_maps = []
    for c in range(NCORES):
        sl = slice(c * 4, (c + 1) * 4)
        in_maps.append({
            "qTs": np.ascontiguousarray(qsf[sl]),
            "kTs": np.ascontiguousarray(ksf[sl]),
            "las": np.ascontiguousarray(lasf[sl]),
            "kts": np.ascontiguousarray(ktsf[sl]),
            "vts": np.ascontiguousarray(vtsf[sl]),
        })
    res2 = run_bass_kernel_spmd(nc2, in_maps, core_ids=list(range(NCORES)))
    osf = np.concatenate([res2.results[c]["os"] for c in range(NCORES)], axis=0)
    osf = osf.reshape(4, 8, 64, NKV)

    nc3 = _get("gla_out", build_gla_out)
    in_maps = []
    for c in range(NCORES):
        b, half = c // 2, c % 2
        sl = slice(half * TOK, (half + 1) * TOK)
        in_maps.append({
            "xT": shards[c],
            "resT": resT[c],
            "os": np.ascontiguousarray(osf[b, :, :, sl]),
            "won": np.ascontiguousarray(ins["L1_attn_a_Wo"]),
        })
    res3 = run_bass_kernel_spmd(nc3, in_maps, core_ids=list(range(NCORES)))
    return _unshard_T([res3.results[c]["yT"] for c in range(NCORES)])



def alloc_quant(pr, W):
    pr.sbuf("qa", [128, W], F32)
    pr.sbuf("qA", [128, W], F32)
    pr.sbuf("qB", [128, W], F32)
    pr.sbuf("qm", [128, W], F32)
    pr.sbuf("qs", [128, W // 16], F32)
    pr.sbuf("qr", [128, W // 16], F32)


def emit_quant(pr, src_fp, dst_bf, W, deps):
    qa = pr.sb["qa"]; qA = pr.sb["qA"]; qB = pr.sb["qB"]
    qm = pr.sb["qm"]; qs = pr.sb["qs"]; qr = pr.sb["qr"]
    NB = W // 16
    out_tok = [None]
    st = [None]

    def f_abs(E):
        E.w(*deps)
        E.w(self_tok(pr, "vector"))
        st[0] = E.t(E.e.activation(qa[:, 0:W], src_fp, AF.Abs))
    pr.sec("scalar", f_abs)

    def f_max(E):
        E.w(st[0])
        t = E.t(E.e.tensor_max(qm[:, 0:W // 2], qa[:, 0:W:2], qa[:, 1:W:2]))
        E.w(t)
        t = E.t(E.e.tensor_max(qm[:, W // 2:W // 2 + W // 4],
                               qm[:, 0:W // 2:2], qm[:, 1:W // 2:2]))
        E.w(t)
        t = E.t(E.e.tensor_max(qm[:, 0:W // 8],
                               qm[:, W // 2:W // 2 + W // 4:2],
                               qm[:, W // 2 + 1:W // 2 + W // 4:2]))
        E.w(t)
        t = E.t(E.e.tensor_max(qm[:, W // 2:W // 2 + NB],
                               qm[:, 0:W // 8:2], qm[:, 1:W // 8:2]))
        E.w(t)
        t1 = E.t(E.e.tensor_scalar(qs[:, 0:NB], qm[:, W // 2:W // 2 + NB],
                                   1.0 / 6.0, 1e-12, ALU.mult, ALU.max))
        E.w(t1)
        t2 = E.t(E.e.reciprocal(qr[:, 0:NB], qs[:, 0:NB]))
        E.w(t2)
        t3 = E.t(E.e.tensor_mul(
            qa[:, 0:W].rearrange("p (b k) -> p b k", k=16),
            src_fp.rearrange("p (b k) -> p b k", k=16),
            qr[:, 0:NB].unsqueeze(2).to_broadcast([128, NB, 16])))
        st[0] = t3
    pr.sec("vector", f_max)

    def f_t(E):
        E.w(st[0])
        st[0] = E.t(E.e.activation(qm[:, 0:W], qa[:, 0:W], AF.Abs))
    pr.sec("scalar", f_t)

    def f_lv(E):
        t_ = qm
        E.w(st[0])
        t = E.t(E.e.tensor_single_scalar(qB[:, 0:W], t_[:, 0:W], 5.0, ALU.is_gt))
        E.w(t)
        t = E.t(E.e.tensor_scalar_mul(qB[:, 0:W], qB[:, 0:W], 2.0))
        E.w(t)
        for thr in (2.5, 3.5):
            t = E.t(E.e.scalar_tensor_tensor(qB[:, 0:W], t_[:, 0:W], thr,
                                             qB[:, 0:W], ALU.is_gt, ALU.add))
            E.w(t)
        t = E.t(E.e.tensor_single_scalar(qA[:, 0:W], t_[:, 0:W], 0.25, ALU.is_gt))
        E.w(t)
        for thr in (0.75, 1.25, 1.75):
            t = E.t(E.e.scalar_tensor_tensor(qA[:, 0:W], t_[:, 0:W], thr,
                                             qA[:, 0:W], ALU.is_gt, ALU.add))
            E.w(t)
        t = E.t(E.e.scalar_tensor_tensor(qA[:, 0:W], qA[:, 0:W], 0.5,
                                         qB[:, 0:W], ALU.mult, ALU.add))
        st[0] = t
    pr.sec("vector", f_lv)

    def f_sg(E):
        E.w(st[0])
        st[0] = E.t(E.e.sign(qm[:, 0:W], qa[:, 0:W]))
    pr.sec("scalar", f_sg)

    def f_fin(E):
        E.w(st[0])
        t1 = E.t(E.e.tensor_mul(qA[:, 0:W], qA[:, 0:W], qm[:, 0:W]))
        E.w(t1)
        t2 = E.t(E.e.tensor_mul(
            dst_bf.rearrange("p (b k) -> p b k", k=16),
            qA[:, 0:W].rearrange("p (b k) -> p b k", k=16),
            qs[:, 0:NB].unsqueeze(2).to_broadcast([128, NB, 16])))
        out_tok[0] = t2
    pr.sec("vector", f_fin)
    return out_tok[0]


def build_l1wq(nrows, W):
    pr = Pr()
    nc = pr.nc
    w_d = nc.dram_tensor("w", [nrows, W], F32, kind="ExternalInput").ap()
    o_d = nc.dram_tensor("o", [nrows, W], BF16, kind="ExternalOutput").ap()
    wt = pr.sbuf("wt", [128, W], F32)
    ob = pr.sbuf("ob", [128, W], BF16)
    alloc_quant(pr, W)
    pr.start()
    starts = list(range(0, nrows - 127, 128))
    if nrows % 128:
        starts.append(nrows - 128)
    for r0 in starts:
        ld = [None]

        def f_ld(E, r0=r0):
            E.w(("vector", pr.cnt["vector"]), ("scalar", pr.cnt["scalar"]))
            ld[0] = E.dma(wt[:, :], w_d[r0:r0 + 128, :])
        pr.sec("sync", f_ld)

        tok = emit_quant(pr, wt[:, :], ob[:, :], W, [ld[0], ("dS", pr.cnt["dS"])])

        def f_st(E, r0=r0, tok=tok):
            E.w(tok)
            E.dma(o_d[r0:r0 + 128, :], ob[:, :])
        pr.sec("sync", f_st)
    return pr.finish()


FSUB = 512
FI = 2816
FSL = (512, 512, 512, 512, 512, 256)


def build_ffn1():
    pr = Pr()
    nc = pr.nc
    xT_d = nc.dram_tensor("xT", [D, TOK], F32, kind="ExternalInput").ap()
    wgT_d = nc.dram_tensor("wgT", [D, FI], BF16, kind="ExternalInput").ap()
    wuT_d = nc.dram_tensor("wuT", [D, FI], BF16, kind="ExternalInput").ap()
    wdT_d = nc.dram_tensor("wdT", [FI, D], BF16, kind="ExternalInput").ap()
    yT_d = nc.dram_tensor("yT", [D, TOK], F32, kind="ExternalOutput").ap()

    pr.sbuf("sxT", [128, KCH * FSUB], F32)
    pr.sbuf("lnf", [128, KCH * FSUB], F32)
    pr.sbuf("wbuf", [128, 22 * D], BF16)
    pr.sbuf("xqT", [128, KCH * FSUB], BF16)
    pr.sbuf("hqT", [128, 22 * FSUB], BF16)
    pr.sbuf("gsl", [128, 512], BF16)
    pr.sbuf("hsl", [128, 512], F32)
    pr.sbuf("hqsl", [128, 512], BF16)
    pr.sbuf("xtok", [128, D], F32)
    pr.sbuf("xq", [128, D], BF16)
    pr.sbuf("outsb", [128, 512], F32)
    alloc_quant(pr, D)
    alloc_ln(pr, "att")
    pr.start()
    ctoks = emit_consts(pr)

    for sub in range(TOK // FSUB):
        _ffn1_sub(pr, sub, xT_d, wgT_d, wuT_d, wdT_d, yT_d, ctoks)
    return pr.finish()


def _ffn1_sub(pr, sub, xT_d, wgT_d, wuT_d, wdT_d, yT_d, ctoks):
    P = pr.ps
    sb = pr.sb
    xT = sb["sxT"]; lnf = sb["lnf"]; wbuf = sb["wbuf"]; xqT = sb["xqT"]
    hqT = sb["hqT"]; gsl = sb["gsl"]; hsl = sb["hsl"]; hqsl = sb["hqsl"]
    xtok = sb["xtok"]; xq = sb["xq"]; outsb = sb["outsb"]
    tok0 = sub * FSUB
    pr.fence()
    ld = [None]

    def f_ld(E, sub=sub):
        ld[0] = E.dma(sb3(xT[:, :], KCH), rearr_kp(xT_d)[:, :, tok0:tok0 + FSUB])
    pr.sec("sync", f_ld)

    ln_toks = emit_ln(pr, xT, lnf, FSUB, [ld[0]] + ctoks, "att")

    for ts in range(FSUB // 128):
        for k in range(KCH):
            tt = [None]

            def f_tr(E, ts=ts, k=k, tt=tt):
                E.w(ln_toks[-1])
                E.w(self_tok(pr, "vector"))
                i = E.e.transpose(P[4 + k % 2][:, 0:128],
                                  lnf[:, k * FSUB + ts * 128: k * FSUB + (ts + 1) * 128],
                                  pr.sb["eye"][:, :])
                tt[0] = E.t(i)
            pr.sec("tensor", f_tr)

            def f_cp(E, ts=ts, k=k, tt=tt):
                E.w(tt[0])
                E.t(E.e.tensor_copy(xtok[:, k * 128:(k + 1) * 128], P[4 + k % 2][:, 0:128]))
            pr.sec("vector", f_cp)

        qtok = emit_quant(pr, xtok[:, :], xq[:, :], D, [self_tok(pr, "vector")])

        for k in range(KCH):
            tt = [None]

            def f_tr2(E, ts=ts, k=k, tt=tt, qtok=qtok):
                E.w(qtok)
                E.w(self_tok(pr, "vector"))
                i = E.e.transpose(P[4 + k % 2].ap().bitcast(BF16)[0:128, 0:128],
                                  xq[:, k * 128:(k + 1) * 128], pr.sb["eyeb"][:, :])
                tt[0] = E.t(i)
            pr.sec("tensor", f_tr2)

            def f_cp2(E, ts=ts, k=k, tt=tt):
                E.w(tt[0])
                E.t(E.e.tensor_copy(xqT[:, k * FSUB + ts * 128: k * FSUB + (ts + 1) * 128],
                                    P[4 + k % 2].ap().bitcast(BF16)[0:128, 0:128]))
            pr.sec("vector", f_cp2)

    pr.fence()

    off = 0
    for osl, w in enumerate(FSL):
        o0 = off
        off += w
        wt = [None]

        def f_ldw(E, o0=o0, w=w):
            E.w(self_tok(pr, "tensor"))
            E.dma(sb3(wbuf[:, 0:KCH * 512], KCH)[:, :, 0:w], rearr_kp(wgT_d)[:, :, o0:o0 + w])
            wt[0] = E.dma(sb3(wbuf[:, KCH * 512:2 * KCH * 512], KCH)[:, :, 0:w],
                          rearr_kp(wuT_d)[:, :, o0:o0 + w])
        pr.sec("gpsimd", f_ldw)

        for ts in range(FSUB // 128):
            gm = [None]

            def f_gmm(E, ts=ts, w=w, gm=gm, wt=wt):
                E.w(wt[0], self_tok(pr, "scalar"))
                for k in range(KCH):
                    i = E.e.matmul(P[4][:, 0:w],
                                   xqT[:, k * FSUB + ts * 128: k * FSUB + (ts + 1) * 128],
                                   wbuf[:, k * 512:k * 512 + w],
                                   start=(k == 0), stop=(k == KCH - 1))
                gm[0] = E.t(i)
            pr.sec("tensor", f_gmm)

            def f_silu(E, ts=ts, w=w, gm=gm):
                E.w(gm[0])
                gm[0] = E.t(E.e.activation(gsl[:, 0:w], P[4][:, 0:w], AF.Silu))
            pr.sec("scalar", f_silu)

            um = [None]

            def f_umm(E, ts=ts, w=w, um=um, gm=gm):
                E.w(gm[0], self_tok(pr, "vector"))
                for k in range(KCH):
                    i = E.e.matmul(P[5][:, 0:w],
                                   xqT[:, k * FSUB + ts * 128: k * FSUB + (ts + 1) * 128],
                                   wbuf[:, KCH * 512 + k * 512: KCH * 512 + k * 512 + w],
                                   start=(k == 0), stop=(k == KCH - 1))
                um[0] = E.t(i)
            pr.sec("tensor", f_umm)

            def f_h(E, ts=ts, w=w, um=um):
                E.w(um[0])
                E.t(E.e.tensor_mul(hsl[:, 0:w], gsl[:, 0:w], P[5][:, 0:w]))
            pr.sec("vector", f_h)

            qtok = emit_quant(pr, hsl[:, 0:w], hqsl[:, 0:w], w, [self_tok(pr, "vector")])

            nk = w // 128
            for k in range(nk):
                tt = [None]

                def f_tr(E, ts=ts, k=k, o0=o0, tt=tt, qtok=qtok):
                    E.w(qtok)
                    E.w(self_tok(pr, "vector"))
                    i = E.e.transpose(P[6].ap().bitcast(BF16)[0:128, 0:128],
                                      hqsl[:, k * 128:(k + 1) * 128], pr.sb["eyeb"][:, :])
                    tt[0] = E.t(i)
                pr.sec("tensor", f_tr)

                def f_cp(E, ts=ts, k=k, o0=o0, tt=tt):
                    E.w(tt[0])
                    kk = (o0 + k * 128) // 128
                    E.t(E.e.tensor_copy(hqT[:, kk * FSUB + ts * 128: kk * FSUB + (ts + 1) * 128],
                                        P[6].ap().bitcast(BF16)[0:128, 0:128]))
                pr.sec("vector", f_cp)

    pr.fence()

    wt = [None]

    def f_ldwd(E):
        wt[0] = E.dma(sb3(wbuf[:, :], 22), wdT_d.rearrange("(k p) n -> p k n", p=128), sem="dG")
    pr.sec("gpsimd", f_ldwd)

    for o in range(KCH):
        dm = [None]

        def f_dmm(E, o=o, dm=dm, wt=wt):
            E.w(wt[0], self_tok(pr, "vector"))
            for k in range(22):
                i = E.e.matmul(P[4 + o % 2][:, :],
                               wbuf[:, k * D + o * 128: k * D + o * 128 + 128],
                               hqT[:, k * FSUB:(k + 1) * FSUB],
                               start=(k == 0), stop=(k == 21))
            dm[0] = E.t(i)
        pr.sec("tensor", f_dmm)

        def f_res(E, o=o, dm=dm):
            E.w(dm[0])
            E.w(("dS", pr.cnt["dS"]))
            E.t(E.e.tensor_tensor(outsb[:, :], xT[:, o * FSUB:(o + 1) * FSUB],
                                  P[4 + o % 2][:, :], ALU.add))
        pr.sec("vector", f_res)

        def f_st(E, o=o, sub=sub):
            E.w(self_tok(pr, "vector"))
            E.dma(rearr_kp(yT_d)[:, o, tok0:tok0 + FSUB], outsb[:, :])
        pr.sec("sync", f_st)


def run_l1wq(ins):
    wg = ins["L1_ffn_w_gate"]; wu = ins["L1_ffn_w_up"]; wd = ins["L1_ffn_w_down"]
    big = np.concatenate([wg, wu], axis=0)
    nc = _get("l1wq_1024", lambda: build_l1wq(704, 1024))
    res = run_bass_kernel_spmd(nc, [{"w": np.ascontiguousarray(big[c * 704:(c + 1) * 704])}
                                    for c in range(NCORES)], core_ids=list(range(NCORES)))
    bigq = np.concatenate([res.results[c]["o"] for c in range(NCORES)], axis=0)
    wgq, wuq = bigq[0:FI], bigq[FI:2 * FI]
    nc2 = _get("l1wq_2816", lambda: build_l1wq(128, FI))
    res2 = run_bass_kernel_spmd(nc2, [{"w": np.ascontiguousarray(wd[c * 128:(c + 1) * 128])}
                                      for c in range(NCORES)], core_ids=list(range(NCORES)))
    wdq = np.concatenate([res2.results[c]["o"] for c in range(NCORES)], axis=0)
    return (np.ascontiguousarray(wgq.T), np.ascontiguousarray(wuq.T),
            np.ascontiguousarray(wdq.T))


def run_ffn1(x_flat, ins, wq3=None):
    nc = _get("ffn1", build_ffn1)
    wgT, wuT, wdT = wq3 if wq3 is not None else run_l1wq(ins)
    shards = _shard_T(x_flat)
    in_maps = [{"xT": s, "wgT": wgT, "wuT": wuT, "wdT": wdT} for s in shards]
    res = run_bass_kernel_spmd(nc, in_maps, core_ids=list(range(NCORES)))
    return _unshard_T([res.results[c]["yT"] for c in range(NCORES)])



RH = 16


def _gammas_np():
    lo = float(np.log(1.0 - 0.999))
    hi = float(np.log(1.0 - 0.9))
    return 1.0 - np.exp(np.linspace(lo, hi, RH, dtype=np.float32))


def build_retqkv():
    pr = Pr()
    nc = pr.nc
    xT_d = nc.dram_tensor("xT", [D, TOK], F32, kind="ExternalInput").ap()
    wqT_d = nc.dram_tensor("wqT", [D, D], F32, kind="ExternalInput").ap()
    wkT_d = nc.dram_tensor("wkT", [D, D], F32, kind="ExternalInput").ap()
    wvT_d = nc.dram_tensor("wvT", [D, D], F32, kind="ExternalInput").ap()
    qT_d = nc.dram_tensor("qTs", [RH, 64, TOK], BF16, kind="ExternalOutput").ap()
    kT_d = nc.dram_tensor("kTs", [RH, 64, TOK], BF16, kind="ExternalOutput").ap()
    vt_d = nc.dram_tensor("vts", [RH, TOK, 64], BF16, kind="ExternalOutput").ap()

    xT = pr.sbuf("sxT", [128, KCH * 512], F32)
    xln = pr.sbuf("xln", [128, KCH * 512], BF16)
    wbuf = pr.sbuf("wbuf", [128, KCH * D], BF16)
    qf = pr.sbuf("qf", [128, 512], BF16)
    ktn = pr.sbuf("ktn", [128, 128], BF16)
    alloc_ln(pr, "att")
    pr.start()
    ctoks = emit_consts(pr)
    P = pr.ps

    for r in range(TOK // 512):
        pr.fence()
        ld = [None]

        def f_ld(E, r=r):
            ld[0] = E.dma(sb3(xT[:, :], KCH), rearr_kp(xT_d)[:, :, r * 512:(r + 1) * 512])
        pr.sec("sync", f_ld)

        ln_toks = emit_ln(pr, xT, xln, 512, [ld[0]] + ctoks, "att")

        for wi, (wd, dst_d, tokmajor) in enumerate(
                ((wqT_d, qT_d, False), (wkT_d, kT_d, False), (wvT_d, vt_d, True))):
            wt = [None]

            def f_ldw(E, wd=wd):
                E.w(self_tok(pr, "tensor"))
                wt[0] = E.dma(sb3(wbuf[:, :], KCH), rearr_kp(wd), sem="dG")
            pr.sec("gpsimd", f_ldw)

            for o in range(KCH):
                pm = [None]

                def f_mm(E, r=r, o=o, wt=wt, pm=pm):
                    E.w(wt[0], ln_toks[-1])
                    E.w(self_tok(pr, "vector"), ("dS", pr.cnt["dS"]))
                    for k in range(KCH):
                        i = E.e.matmul(P[4][:, :],
                                       wbuf[:, k * D + o * 128: k * D + o * 128 + 128],
                                       xln[:, k * 512:(k + 1) * 512],
                                       start=(k == 0), stop=(k == KCH - 1))
                    pm[0] = E.t(i)
                pr.sec("tensor", f_mm)

                def f_ev(E, r=r, o=o, pm=pm):
                    E.w(pm[0])
                    pm[0] = E.t(E.e.tensor_copy(qf[:, :], P[4][:, :]))
                pr.sec("vector", f_ev)

                if not tokmajor:
                    def f_st(E, r=r, o=o, dst_d=dst_d, pm=pm):
                        E.w(pm[0])
                        for hh in range(2):
                            h = o * 2 + hh
                            E.dma(dst_d[h, :, r * 512:(r + 1) * 512],
                                  qf[hh * 64:(hh + 1) * 64, :])
                    pr.sec("sync", f_st)
                else:
                    for ts in range(4):
                        tt = [None]

                        def f_tr(E, o=o, ts=ts, tt=tt, pm=pm):
                            E.w(pm[0], self_tok(pr, "vector"), ("dS", pr.cnt["dS"]))
                            i = E.e.transpose(P[5].ap().bitcast(BF16)[0:128, 0:128],
                                              qf[:, ts * 128:(ts + 1) * 128],
                                              pr.sb["eyeb"][:, :])
                            tt[0] = E.t(i)
                        pr.sec("tensor", f_tr)

                        def f_trc(E, o=o, ts=ts, tt=tt):
                            E.w(tt[0])
                            tt[0] = E.t(E.e.tensor_copy(ktn[:, :],
                                                        P[5].ap().bitcast(BF16)[0:128, 0:128]))
                        pr.sec("vector", f_trc)

                        def f_st2(E, r=r, o=o, ts=ts, dst_d=dst_d, tt=tt):
                            E.w(tt[0])
                            for hh in range(2):
                                h = o * 2 + hh
                                E.dma(dst_d[h, r * 512 + ts * 128: r * 512 + (ts + 1) * 128, :],
                                      ktn[:, hh * 64:(hh + 1) * 64])
                        pr.sec("sync", f_st2)
    return pr.finish()


def build_retscan():
    pr = Pr()
    nc = pr.nc
    qT_d = nc.dram_tensor("qTs", [8, 64, NKV], BF16, kind="ExternalInput").ap()
    kT_d = nc.dram_tensor("kTs", [8, 64, NKV], BF16, kind="ExternalInput").ap()
    vt_d = nc.dram_tensor("vts", [8, NKV, 64], BF16, kind="ExternalInput").ap()
    dm_d = nc.dram_tensor("dmask", [128, 8 * 128], F32, kind="ExternalInput").ap()
    qg_d = nc.dram_tensor("qgrow", [64, 8 * 128], F32, kind="ExternalInput").ap()
    kg_d = nc.dram_tensor("kgcol", [128, 8], F32, kind="ExternalInput").ap()
    g128_d = nc.dram_tensor("g128col", [64, 8], F32, kind="ExternalInput").ap()
    gnw_d = nc.dram_tensor("gnwcol", [64, 8], F32, kind="ExternalInput").ap()
    gnb_d = nc.dram_tensor("gnbcol", [64, 8], F32, kind="ExternalInput").ap()
    y_d = nc.dram_tensor("ys", [8, 64, NKV], BF16, kind="ExternalOutput").ap()

    qT = pr.sbuf("qT", [64, 8 * NKV // 8], BF16)
    kT = pr.sbuf("kT", [64, NKV], BF16)
    vt = pr.sbuf("vt", [128, 32 * 64], BF16)
    yb = pr.sbuf("yb", [64, NKV], BF16)
    dm = pr.sbuf("dm", [128, 8 * 128], BF16)
    qg = pr.sbuf("qg", [64, 8 * 128], BF16)
    kg = pr.sbuf("kg", [128, 8], F32)
    g128 = pr.sbuf("g128", [64, 8], F32)
    gnw = pr.sbuf("gnw", [64, 8], F32)
    gnb = pr.sbuf("gnb", [64, 8], F32)
    qgb = pr.sbuf("qgb", [64, 128], BF16)
    kgb = pr.sbuf("kgb", [128, 64], BF16)
    PTm = pr.sbuf("PTm", [128, 128], BF16)
    S = pr.sbuf("Srt", [64, 64], BF16)
    sacc = pr.sbuf("sacc", [64, 2], F32)
    stat = pr.sbuf("stat", [64, 8], F32)
    srow = pr.sbuf("srow", [1, 128], F32)
    mrr = pr.sbuf("mrr", [1, 4], F32)
    sq64 = pr.sbuf("sq64", [64, 128], BF16)
    pr.start()
    ctoks = emit_consts(pr)
    P = pr.ps

    def f_i(E):
        ctoks.append(E.dma(dm[:, :], dm_d, sem="dG"))
        ctoks.append(E.dma(qg[:, :], qg_d, sem="dG"))
    pr.sec("gpsimd", f_i)

    def f_i2(E):
        ctoks.append(E.dma(kg[:, :], kg_d))
        ctoks.append(E.dma(g128[:, :], g128_d))
        ctoks.append(E.dma(gnw[:, :], gnw_d))
        ctoks.append(E.dma(gnb[:, :], gnb_d))
    pr.sec("sync", f_i2)

    for u in range(8):
        _ret_unit(pr, u, qT_d, kT_d, vt_d, y_d, ctoks)
    return pr.finish()


def _ret_unit(pr, u, qT_d, kT_d, vt_d, y_d, ctoks):
    P = pr.ps
    sb = pr.sb
    qT = sb["qT"]; kT = sb["kT"]; vt = sb["vt"]; yb = sb["yb"]
    dm = sb["dm"]; qg = sb["qg"]; kg = sb["kg"]; g128 = sb["g128"]
    gnw = sb["gnw"]; gnb = sb["gnb"]; qgb = sb["qgb"]; kgb = sb["kgb"]
    PTm = sb["PTm"]; S = sb["Srt"]; sacc = sb["sacc"]; stat = sb["stat"]
    srow = sb["srow"]; mrr = sb["mrr"]; sq64 = sb["sq64"]
    pr.fence()
    ld = [None]

    def f_ld(E, u=u):
        E.dma(qT[:, 0:NKV], qT_d[u, :, :])
        E.dma(kT[:, :], kT_d[u, :, :])
        ld[0] = E.dma(sb3(vt[:, :], 32), vt_d[u, :, :].rearrange("(r p) d -> p r d", p=128))
    pr.sec("sync", f_ld)

    def f_z(E, u=u):
        E.t(E.e.memset(S[:, :], 0.0))
        E.t(E.e.memset(sacc[:, :], 0.0))
    pr.sec("gpsimd", f_z)

    for j in range(32):
        pt = [None]

        def f_pt(E, u=u, j=j, pt=pt):
            E.w(ld[0], *ctoks)
            E.w(self_tok(pr, "vector"), self_tok(pr, "gpsimd"))
            i = E.e.matmul(P[0][:, 0:128],
                           kT[:, j * 128:(j + 1) * 128],
                           qT[:, j * 128:(j + 1) * 128], start=True, stop=True)
            pt[0] = E.t(i)
        pr.sec("tensor", f_pt)

        def f_ptm(E, u=u, j=j, pt=pt):
            E.w(pt[0])
            t1 = E.t(E.e.tensor_mul(PTm[:, :], P[0][:, 0:128], dm[:, u * 128:(u + 1) * 128]))
            E.w(t1)
            t2 = E.t(E.e.tensor_mul(qgb[:, :], qT[:, j * 128:(j + 1) * 128],
                                    qg[:, u * 128:(u + 1) * 128]))
            E.w(t2)
            pt[0] = t2
        pr.sec("vector", f_ptm)

        om = [None]

        def f_om(E, u=u, j=j, pt=pt, om=om):
            E.w(pt[0])
            E.e.matmul(P[1][0:64, 0:128],
                       vt[:, j * 64:(j + 1) * 64], PTm[:, :],
                       start=True, stop=False, skip_group_check=True)
            i = E.e.matmul(P[1][0:64, 0:128], S[:, :], qgb[:, :],
                           start=False, stop=True, skip_group_check=True)
            om[0] = E.t(i)
        pr.sec("tensor", f_om)

        def f_oev(E, u=u, j=j, om=om):
            E.w(om[0])
            om[0] = E.t(E.e.tensor_copy(yb[:, j * 128:(j + 1) * 128], P[1][0:64, 0:128]))
        pr.sec("vector", f_oev)

        ktt = [None]

        def f_ktr(E, u=u, j=j, ktt=ktt):
            E.w(self_tok(pr, "vector"))
            i = E.e.transpose(P[2].ap().bitcast(BF16)[0:128, 0:64],
                              kT[:, j * 128:(j + 1) * 128], pr.sb["eyeb"][0:64, 0:64])
            ktt[0] = E.t(i)
        pr.sec("tensor", f_ktr)

        def f_ktc(E, u=u, j=j, ktt=ktt):
            E.w(ktt[0])
            ktt[0] = E.t(E.e.tensor_scalar_mul(kgb[:, :],
                                               P[2].ap().bitcast(BF16)[0:128, 0:64],
                                               kg[:, u:u + 1]))
        pr.sec("vector", f_ktc)

        um = [None]

        def f_um(E, u=u, j=j, ktt=ktt, um=um):
            E.w(ktt[0])
            i = E.e.matmul(P[3][0:64, 0:64], kgb[:, :], vt[:, j * 64:(j + 1) * 64],
                           start=True, stop=True)
            um[0] = E.t(i)
        pr.sec("tensor", f_um)

        def f_sup(E, u=u, j=j, um=um):
            E.w(um[0])
            E.t(E.e.scalar_tensor_tensor(S[:, :], S[:, :], g128[:, u:u + 1],
                                         P[3][0:64, 0:64], ALU.mult, ALU.add))
        pr.sec("vector", f_sup)

        def f_gn1(E, u=u, j=j):
            E.w(self_tok(pr, "vector"))
            E.t(E.e.activation(sq64[:, :], yb[:, j * 128:(j + 1) * 128], AF.Square))
        pr.sec("scalar", f_gn1)

        def f_gn2(E, u=u, j=j):
            E.w(self_tok(pr, "scalar"))
            t1 = E.t(E.e.reduce_sum(stat[:, 0:1], yb[:, j * 128:(j + 1) * 128], axis=AX.X))
            E.w(t1)
            t2 = E.t(E.e.reduce_sum(stat[:, 1:2], sq64[:, :], axis=AX.X))
            E.w(t2)
            t3 = E.t(E.e.tensor_add(sacc[:, 0:1], sacc[:, 0:1], stat[:, 0:1]))
            E.w(t3)
            t4 = E.t(E.e.tensor_add(sacc[:, 1:2], sacc[:, 1:2], stat[:, 1:2]))
            E.w(t4)
        pr.sec("vector", f_gn2)

    fz = [None]

    def f_ftr(E, u=u, fz=fz):
        E.w(self_tok(pr, "vector"))
        E.e.transpose(P[4][0:1, 0:64], sacc[:, 0:1], pr.sb["eye"][0:64, 0:64])
        i = E.e.transpose(P[4][0:1, 64:128], sacc[:, 1:2], pr.sb["eye"][0:64, 0:64])
        fz[0] = E.t(i)
    pr.sec("tensor", f_ftr)

    def f_fred(E, u=u, fz=fz):
        E.w(fz[0])
        t1 = E.t(E.e.tensor_copy(srow[:, 0:64], P[4][0:1, 0:64]))
        E.w(t1)
        t2 = E.t(E.e.reduce_sum(mrr[:, 0:1], srow[:, 0:64], axis=AX.X))
        E.w(t2)
        fz[0] = t2
    pr.sec("vector", f_fred)

    def f_fred2(E, u=u, fz=fz):
        E.w(fz[0])
        t1 = E.t(E.e.tensor_copy(srow[:, 0:64], P[4][0:1, 64:128]))
        E.w(t1)
        t2 = E.t(E.e.reduce_sum(mrr[:, 1:2], srow[:, 0:64], axis=AX.X))
        E.w(t2)
        t3 = E.t(E.e.tensor_scalar_mul(mrr[:, 0:1], mrr[:, 0:1], 1.0 / (NKV * 64.0)))
        E.w(t3)
        t4 = E.t(E.e.tensor_scalar_mul(mrr[:, 1:2], mrr[:, 1:2], 1.0 / (NKV * 64.0)))
        E.w(t4)
        t5 = E.t(E.e.tensor_mul(mrr[:, 2:3], mrr[:, 0:1], mrr[:, 0:1]))
        E.w(t5)
        t6 = E.t(E.e.tensor_sub(mrr[:, 1:2], mrr[:, 1:2], mrr[:, 2:3]))
        E.w(t6)
        t7 = E.t(E.e.tensor_scalar(mrr[:, 1:2], mrr[:, 1:2], 1e-5, None, ALU.add))
        fz[0] = t7
    pr.sec("vector", f_fred2)

    def f_fs(E, u=u, fz=fz):
        E.w(fz[0])
        fz[0] = E.t(E.e.activation(mrr[:, 1:2], mrr[:, 1:2], AF.Sqrt))
    pr.sec("scalar", f_fs)

    def f_fr(E, u=u, fz=fz):
        E.w(fz[0])
        t1 = E.t(E.e.reciprocal(mrr[:, 1:2], mrr[:, 1:2]))
        fz[0] = t1
    pr.sec("vector", f_fr)

    def f_fbc(E, u=u, fz=fz):
        E.w(fz[0])
        E.e.matmul(P[5][0:64, 0:1], pr.sb["ones1r"][:, 0:64], mrr[:, 0:1],
                   start=True, stop=True)
        i = E.e.matmul(P[5][0:64, 1:2], pr.sb["ones1r"][:, 0:64], mrr[:, 1:2],
                       start=True, stop=True)
        fz[0] = E.t(i)
    pr.sec("tensor", f_fbc)

    def f_fcp(E, u=u, fz=fz):
        E.w(fz[0])
        t1 = E.t(E.e.tensor_copy(stat[:, 2:3], P[5][0:64, 0:1]))
        E.w(t1)
        t2 = E.t(E.e.tensor_copy(stat[:, 3:4], P[5][0:64, 1:2]))
        E.w(t2)
        fz[0] = t2
    pr.sec("vector", f_fcp)

    for j in range(32):
        def f_nap(E, u=u, j=j, fz=fz):
            E.w(fz[0])
            t1 = E.t(E.e.tensor_scalar(yb[:, j * 128:(j + 1) * 128],
                                       yb[:, j * 128:(j + 1) * 128],
                                       stat[:, 2:3], stat[:, 3:4],
                                       ALU.subtract, ALU.mult))
            E.w(t1)
            t2 = E.t(E.e.tensor_scalar(yb[:, j * 128:(j + 1) * 128],
                                       yb[:, j * 128:(j + 1) * 128],
                                       gnw[:, u:u + 1], gnb[:, u:u + 1],
                                       ALU.mult, ALU.add))
            E.w(t2)
        pr.sec("vector", f_nap)

    def f_st(E, u=u):
        E.w(self_tok(pr, "vector"))
        E.dma(y_d[u, :, :], yb[:, :])
    pr.sec("sync", f_st)


def build_retout():
    pr = Pr()
    nc = pr.nc
    xT_d = nc.dram_tensor("xT", [D, TOK], F32, kind="ExternalInput").ap()
    y_d = nc.dram_tensor("ys", [RH, 64, TOK], BF16, kind="ExternalInput").ap()
    woT_d = nc.dram_tensor("woT", [D, D], F32, kind="ExternalInput").ap()
    out_d = nc.dram_tensor("outT", [D, TOK], F32, kind="ExternalOutput").ap()

    xT = pr.sbuf("sxT", [128, KCH * 512], F32)
    yt = pr.sbuf("yt", [128, KCH * TOK], BF16)
    wbuf = pr.sbuf("wbuf", [128, KCH * D], BF16)
    outsb = pr.sbuf("outsb", [128, 512], F32)
    pr.start()
    ctoks = emit_consts(pr)
    P = pr.ps
    wt = [None]
    olt = [None]

    def f_ldw(E):
        wt[0] = E.dma(sb3(wbuf[:, :], KCH), rearr_kp(woT_d), sem="dG")
    pr.sec("gpsimd", f_ldw)

    def f_ldo(E):
        olt[0] = E.dma(yt[:, :].rearrange("p (k t) -> p k t", k=KCH),
                       y_d.rearrange("(k hh) d t -> (hh d) k t", k=KCH))
    pr.sec("sync", f_ldo)

    for g in range(4):
        xg = [None]

        def f_ldx(E, g=g):
            E.w(("dS", pr.cnt["dS"]))
            xg[0] = E.dma(sb3(xT[:, :], KCH), rearr_kp(xT_d)[:, :, g * 512:(g + 1) * 512])
        pr.sec("sync", f_ldx)

        for o in range(KCH):
            om = [None]

            def f_omm(E, g=g, o=o, om=om):
                E.w(wt[0], olt[0], self_tok(pr, "vector"))
                for k in range(KCH):
                    i = E.e.matmul(P[4 + o % 2][:, :],
                                   wbuf[:, k * D + o * 128: k * D + o * 128 + 128],
                                   yt[:, k * TOK + g * 512: k * TOK + (g + 1) * 512],
                                   start=(k == 0), stop=(k == KCH - 1))
                om[0] = E.t(i)
            pr.sec("tensor", f_omm)

            def f_oadd(E, g=g, o=o, om=om):
                E.w(om[0], xg[0])
                E.w(("dS", pr.cnt["dS"]))
                E.t(E.e.tensor_tensor(outsb[:, :], xT[:, o * 512:(o + 1) * 512],
                                      P[4 + o % 2][:, :], ALU.add))
            pr.sec("vector", f_oadd)

            def f_ost(E, g=g, o=o):
                E.w(self_tok(pr, "vector"))
                E.dma(rearr_kp(out_d)[:, o, g * 512:(g + 1) * 512], outsb[:, :])
            pr.sec("sync", f_ost)
    return pr.finish()


def run_ret(x_flat, ins):
    import ml_dtypes
    bf = ml_dtypes.bfloat16
    nc1 = _get("retqkv", build_retqkv)
    shards = _shard_T(x_flat)
    base = {
        "wqT": np.ascontiguousarray(ins["L1_attn_b_wq"].T),
        "wkT": np.ascontiguousarray(ins["L1_attn_b_wk"].T),
        "wvT": np.ascontiguousarray(ins["L1_attn_b_wv"].T),
    }
    res = run_bass_kernel_spmd(nc1, [dict(base, xT=s) for s in shards],
                               core_ids=list(range(NCORES)))
    qs = np.zeros((4, RH, 64, NKV), dtype=bf)
    ks = np.zeros((4, RH, 64, NKV), dtype=bf)
    vs = np.zeros((4, RH, NKV, 64), dtype=bf)
    for c in range(NCORES):
        b, half = c // 2, c % 2
        r = res.results[c]
        sl = slice(half * TOK, (half + 1) * TOK)
        qs[b, :, :, sl] = r["qTs"]
        ks[b, :, :, sl] = r["kTs"]
        vs[b, :, sl, :] = r["vts"]

    gam = _gammas_np()
    nc2 = _get("retscan", build_retscan)
    qsf = qs.reshape(64, 64, NKV)
    ksf = ks.reshape(64, 64, NKV)
    vsf = vs.reshape(64, NKV, 64)
    gnw_full = ins["L1_attn_b_gn_w"].reshape(RH, 64)
    gnb_full = ins["L1_attn_b_gn_b"].reshape(RH, 64)
    in_maps = []
    for c in range(NCORES):
        units = [(uu // RH, uu % RH) for uu in range(c * 8, (c + 1) * 8)]
        dmask = np.zeros((128, 8 * 128), np.float32)
        qgrow = np.zeros((64, 8 * 128), np.float32)
        kgcol = np.zeros((128, 8), np.float32)
        g128c = np.zeros((64, 8), np.float32)
        gnwc = np.zeros((64, 8), np.float32)
        gnbc = np.zeros((64, 8), np.float32)
        for i, (b, h) in enumerate(units):
            g = gam[h]
            idx = np.arange(128)
            dmask[:, i * 128:(i + 1) * 128] = np.where(
                idx[None, :] >= idx[:, None],
                g ** (idx[None, :] - idx[:, None]), 0.0)
            qgrow[:, i * 128:(i + 1) * 128] = (g ** (idx + 1.0))[None, :]
            kgcol[:, i] = g ** (127.0 - idx)
            g128c[:, i] = g ** 128.0
            gnwc[:, i] = gnw_full[h]
            gnbc[:, i] = gnb_full[h]
        sl = slice(c * 8, (c + 1) * 8)
        in_maps.append({
            "qTs": np.ascontiguousarray(qsf[sl]),
            "kTs": np.ascontiguousarray(ksf[sl]),
            "vts": np.ascontiguousarray(vsf[sl]),
            "dmask": dmask, "qgrow": qgrow, "kgcol": kgcol,
            "g128col": g128c, "gnwcol": gnwc, "gnbcol": gnbc,
        })
    res2 = run_bass_kernel_spmd(nc2, in_maps, core_ids=list(range(NCORES)))
    ysf = np.concatenate([res2.results[c]["ys"] for c in range(NCORES)], axis=0)
    ysf = ysf.reshape(4, RH, 64, NKV)

    nc3 = _get("retout", build_retout)
    in_maps = []
    for c in range(NCORES):
        b, half = c // 2, c % 2
        sl = slice(half * TOK, (half + 1) * TOK)
        in_maps.append({
            "xT": shards[c],
            "ys": np.ascontiguousarray(ysf[b, :, :, sl]),
            "woT": np.ascontiguousarray(ins["L1_attn_b_wo"].T),
        })
    res3 = run_bass_kernel_spmd(nc3, in_maps, core_ids=list(range(NCORES)))
    return _unshard_T([res3.results[c]["outT"] for c in range(NCORES)])


def kernel(**inputs):
    ins = {k: np.asarray(v) for k, v in inputs.items()}
    x = np.ascontiguousarray(ins["L0_attn_a_x"].astype(np.float32).reshape(-1, D))
    x = run_atta(x, ins)
    x = run_ffn0(x, ins)
    x = run_perf(x, ins)
    x = run_ffn0(x, ins)
    x = run_gla(x, ins)
    wq3 = run_l1wq(ins)
    x = run_ffn1(x, ins, wq3)
    x = run_ret(x, ins)
    x = run_ffn1(x, ins, wq3)
    return x.reshape(4, NKV, D).astype(np.float32)
